# version 08
# baseline (speedup 1.0000x reference)
import contextlib
import numpy as np

import concourse.bass as bass
import concourse.mybir as mybir
from concourse.bass_utils import run_bass_kernel_spmd

F32 = mybir.dt.float32
I32 = mybir.dt.int32
AF = mybir.ActivationFunctionType
ALU = mybir.AluOpType
ENGINES = ("sync", "tensor", "vector", "scalar", "gpsimd")
RG2 = [[0, 1], [2, 3], [4, 5], [6, 7]]


def make_cfg(D=1024, T=2048, H=16, L=4, with_tail=True):
    DH, NST = 64, 64
    assert D == H * DH and H % 2 == 0
    NT = min(512, T)
    return dict(
        D=D, T=T, H=H, L=L, NST=NST, DH=DH,
        DI=2 * D, DF=4 * D, NT=NT,
        SCALE=1.0 / np.sqrt(DH), EPS=1e-5,
        DT=D // 128, DIT=(2 * D) // 128, DFT=(4 * D) // 128,
        TC=T // NT, NP=H // 2, KC=T // 128, QS=T // NT,
        with_tail=with_tail,
    )


def w_image_omajor(W):
    O, D = W.shape
    A = np.asarray(W, np.float32).reshape(O // 128, 128, D // 128, 128)
    return np.ascontiguousarray(A.transpose(0, 3, 2, 1).reshape(O, D))


def w_image_dmajor(W):
    return np.ascontiguousarray(np.asarray(W, np.float32).T)


def wxp_image(Wxp, DIT, NST):
    A = np.asarray(Wxp, np.float32).T.reshape(DIT, 128, NST)
    return np.ascontiguousarray(A.transpose(1, 0, 2).reshape(128, DIT * NST))


def build_cols_image(cfg, wd):
    cols, idx = [], {}

    def add(name, vec):
        v = np.asarray(vec, np.float32).ravel()
        n = (len(v) + 127) // 128
        idx[name] = len(cols)
        a = np.zeros((n * 128,), np.float32)
        a[: len(v)] = v
        for r in a.reshape(n, 128):
            cols.append(r)

    D = cfg["D"]
    for l in range(cfg["L"]):
        add(f"b_in{l}", wd[f"b_in{l}"])
        add(f"b_dt{l}", wd[f"b_dt{l}"])
        add(f"b_xp{l}", wd[f"b_xp{l}"])
        add(f"b_out{l}", wd[f"b_out{l}"])
        add(f"b_q{l}", wd[f"b_qkv{l}"][:D])
        add(f"b_k{l}", wd[f"b_qkv{l}"][D:2 * D])
        add(f"b_ao{l}", wd[f"b_ao{l}"])
        add(f"b_f1{l}", wd[f"b_f1{l}"])
        add(f"b_f2{l}", wd[f"b_f2{l}"])
        add(f"g1{l}", wd[f"g1{l}"])
        add(f"be1{l}", wd[f"be1{l}"])
        add(f"g2{l}", wd[f"g2{l}"])
        add(f"be2{l}", wd[f"be2{l}"])
        add(f"Dp1{l}", 1.0 + np.asarray(wd[f"Dp{l}"], np.float32))
        add(f"Alog{l}", wd[f"Alog{l}"])
    if cfg["with_tail"]:
        add("fb_r", wd["fbias_r"])
        add("fb_i", wd["fbias_i"])
    return np.ascontiguousarray(np.stack(cols, 1)), idx


def g1_image(G1):
    out = np.zeros((8, 512), np.float32)
    for t in range(4):
        for cc in range(4):
            for p in range(8):
                out[p, (t * 4 + cc) * 32:(t * 4 + cc) * 32 + 32] = G1[0, :, cc, 8 * t + p]
    return out


def g2_image(G2):
    out = np.zeros((16, 64), np.float32)
    for cc in range(4):
        out[:, cc * 16: cc * 16 + 16] = np.asarray(G2, np.float32)[cc, :, 0, :].T
    return out


def host_prepare_weights(cfg, wd):
    D, L = cfg["D"], cfg["L"]
    m = {}
    for l in range(L):
        m[f"wi{l}"] = w_image_omajor(wd[f"W_in{l}"])
        m[f"wdt{l}"] = w_image_omajor(wd[f"W_dt{l}"])
        m[f"wxp{l}"] = wxp_image(wd[f"W_xp{l}"], cfg["DIT"], cfg["NST"])
        m[f"wout{l}"] = w_image_dmajor(wd[f"W_out{l}"])
        m[f"wq{l}"] = w_image_omajor(wd[f"W_qkv{l}"][:D])
        m[f"wk{l}"] = w_image_omajor(wd[f"W_qkv{l}"][D:2 * D])
        m[f"wv{l}"] = w_image_omajor(wd[f"W_qkv{l}"][2 * D:])
        m[f"wao{l}"] = w_image_omajor(wd[f"W_ao{l}"])
        m[f"wf1{l}"] = w_image_omajor(wd[f"W_f1{l}"])
        m[f"wf2{l}"] = w_image_omajor(wd[f"W_f2{l}"])
    img, idx = build_cols_image(cfg, wd)
    m["colsimg"] = img
    m["bvrows"] = np.ascontiguousarray(
        np.stack([np.asarray(wd[f"b_qkv{l}"][2 * D:], np.float32) for l in range(L)], 0))
    if cfg["with_tail"]:
        m["g1r"] = g1_image(wd["G1r"]); m["g2r"] = g2_image(wd["G2r"])
        m["g1i"] = g1_image(wd["G1i"]); m["g2i"] = g2_image(wd["G2i"])
        r8 = np.kron(np.eye(8, dtype=np.float32), np.ones((1, 16), np.float32))
        r16 = np.kron(np.ones((1, 8), np.float32), np.eye(16, dtype=np.float32))
        m["r8r16"] = np.ascontiguousarray(np.concatenate([r8, r16], 0))
    return m, idx


class Builder:

    def __init__(self, cfg, colidx, stage="full"):
        self.cfg = cfg
        self.colidx = colidx
        self.stage = stage
        self.nc = bass.Bass("TRN2", target_bir_lowering=False)
        self.ops = {e: [] for e in ENGINES}
        self.sh, self.sc = {}, {}
        self.d = {}
        self.hold = {}
        self.prev_tok = {}
        self.dlast = {}
        self.bank_free = [None] * 8

    @staticmethod
    def _flat(waits):
        out = []
        for w in waits:
            if w is None:
                continue
            if isinstance(w, list):
                out.extend(x for x in w if x is not None)
            else:
                out.append(w)
        return tuple(out)

    ESEM = {"tensor": "tp", "vector": "vp", "scalar": "sp", "gpsimd": "gp"}

    def op(self, eng, fn, waits=(), inc=None, amt=1, is_dma=False):
        ws = self._flat(waits)
        chain = (not is_dma) and eng in ("vector", "scalar", "gpsimd")
        if chain:
            prev = self.prev_tok.get(eng)
            if prev is not None:
                ws = ws + (prev,)
            if inc is None:
                inc = self.ESEM[eng]
        tok = None
        if inc is not None:
            self.sc[inc] += amt
            tok = (inc, self.sc[inc])
        if chain:
            self.prev_tok[eng] = tok
        if getattr(self, "t_zero", None) and eng != "vector":
            ws = ws + tuple(self.t_zero)
        self.ops[eng].append((fn, ws, inc, tok, amt))
        return tok

    def realize(self, block):
        for eng in ENGINES:
            ops = self.ops[eng]
            if not ops:
                continue

            def make_body(ops):
                def body(e):
                    hi = {}
                    for fn, waits, inc, tok, amt in ops:
                        for s, v in waits:
                            if hi.get(s, -1) < v:
                                e.wait_ge(self.sh[s], v)
                                hi[s] = v
                        inst = fn(e)
                        if inc is not None:
                            inst.then_inc(self.sh[inc], amt)
                return body
            getattr(block, eng)(make_body(ops))

    def dsem(self):
        s = self.dpool[self.di % len(self.dpool)]
        self.di += 1
        return s

    def dma(self, q, out, in_, waits=(), sem=None):
        sem = sem or self.dsem()
        prev = self.dlast.get(sem)
        tok = self.op(q, lambda e: e.dma_start(out=out, in_=in_),
                      waits=self._flat(waits) + ((prev,) if prev else ()),
                      inc=sem, amt=16, is_dma=True)
        self.dlast[sem] = tok
        return tok

    def bcast(self, row_ap, dst_ap, npart, waits=()):
        sem = self.dsem()
        prev = self.dlast.get(sem)
        tok = None
        for qq in range(npart // 32):
            seed = dst_ap[qq * 32: qq * 32 + 1, :]
            tok = self.op(
                "scalar", lambda e, s=seed: e.dma_start(out=s, in_=row_ap),
                waits=(list(waits) + self.t_zero + ([prev] if prev else []))
                if qq == 0 else (),
                inc=sem, amt=16, is_dma=True)
        self.dlast[sem] = tok
        return self.op(
            "vector", lambda e: e.stream_shuffle(out=dst_ap, in_=dst_ap, mask=[0] * 32),
            waits=[tok], inc="vp")

    def load_slab(self, src_ap, ncols, waits=()):
        s = self.wslots[self.wi % len(self.wslots)]
        self.wi += 1
        tok = self.dma("sync", s["ap"][:, :ncols], src_ap,
                       waits=tuple(s["free"]) + tuple(waits))
        s["loaded"] = tok
        s["free"] = []
        return s

    def mm(self, out, lhsT, rhs, start, stop, waits=(), inc=None, tp=None, skip=False):
        kw = {}
        if tp is not None:
            kw["tile_position"] = tp
        if skip:
            kw["skip_group_check"] = True
        return self.op(
            "tensor",
            lambda e: e.matmul(out, lhsT, rhs, start=start, stop=stop, **kw),
            waits=waits, inc=inc)

    def evac(self, bank_i, fn, waits, eng="scalar"):
        tok = self.op(eng, fn, waits=waits, inc="sp" if eng == "scalar" else "vp")
        self.bank_free[bank_i] = tok
        return tok

    def col(self, name, j=0):
        i = self.colidx[name] + j
        return self.SM[:, i: i + 1]

    def trr(self, off, n):
        assert off + n <= self.TRN, (off, n)
        return self.TR[:, off: off + n]

    def row(self, strip, sl=None, np_=1):
        T = self.cfg["T"]
        if sl is None:
            return self.ROWS[0:np_, strip * T: strip * T + T]
        a, b_, st = sl.indices(T)
        return self.ROWS[0:np_, strip * T + a: strip * T + b_]

    def build(self):
        c = self.cfg
        D, T, L, NST = c["D"], c["T"], c["L"], c["NST"]
        DI, DF, NT = c["DI"], c["DF"], c["NT"]
        DT, DIT = c["DT"], c["DIT"]
        nc = self.nc

        din = lambda n, sh, dt=F32: self.d.setdefault(
            n, nc.dram_tensor(n, list(sh), dt, kind="ExternalInput").ap())
        din("x_in", [D, T])
        for l in range(L):
            din(f"wi{l}", [2 * DI, D]); din(f"wdt{l}", [D, DI])
            din(f"wxp{l}", [128, DIT * NST]); din(f"wout{l}", [DI, D])
            din(f"wq{l}", [D, D]); din(f"wk{l}", [D, D]); din(f"wv{l}", [D, D])
            din(f"wao{l}", [D, D]); din(f"wf1{l}", [DF, D]); din(f"wf2{l}", [D, DF])
        din("colsimg", [128, c["ncol"]])
        din("bvrows", [L, D])
        if c["with_tail"]:
            din("g1r", [8, 512]); din("g2r", [16, 64])
            din("g1i", [8, 512]); din("g2i", [16, 64])
            din("r8r16", [24, 128])
        din("maskcol", [NST, 1])
        din("pinfo", [1, 1], I32)
        self.y = nc.dram_tensor("y", [D, T], F32, kind="ExternalOutput").ap()

        self.kv_loc = nc.dram_tensor("kv_loc", [2 * D, T], F32)
        self.kv_all = nc.dram_tensor("kv_all", [2, 2 * D, T], F32)
        self.cr_loc = nc.dram_tensor("cr_loc", [NST, 1], F32)
        self.cr_all = nc.dram_tensor("cr_all", [2, NST, 1], F32)
        self.q_spill = nc.dram_tensor("q_spill", [D, T], F32)
        self.n_spill = nc.dram_tensor("n_spill", [2 * c["NP"], T], F32)

        with contextlib.ExitStack() as ctx:
            self.XA = ctx.enter_context(nc.sbuf_tensor("XA", [128, DT * T], F32))
            xbn = max(DT * T, (DF // 128) * NT)
            self.XB = ctx.enter_context(nc.sbuf_tensor("XB", [128, xbn], F32))
            mnt = min(256, T)
            xi_cols = (DI // 128) * mnt
            self.TRN = max(xi_cols + 2 * T + NT, 5 * T + 4 * NT,
                           (7680 + 2 * T) if c["with_tail"] else 0)
            self.TR = ctx.enter_context(nc.sbuf_tensor("TR", [128, self.TRN], F32))
            self.WB = ctx.enter_context(nc.sbuf_tensor("WB", [128, 2048], F32))
            smn = c["ncol"] + 2 * T + 2 * NT + 64
            self.SM = ctx.enter_context(nc.sbuf_tensor("SM", [128, smn], F32))
            for s in ("tp", "vp", "sp", "gp"):
                self.sh[s] = ctx.enter_context(nc.semaphore(s))
                self.sc[s] = 0
            self.dpool = []
            for i in range(84):
                nm = f"d{i}"
                self.sh[nm] = ctx.enter_context(nc.semaphore(nm))
                self.sc[nm] = 0
                self.dpool.append(nm)
            self.di = 0
            self.ccpool = []
            for i in range(12):
                nm = f"cc{i}"
                self.sh[nm] = ctx.enter_context(nc.semaphore(nm))
                self.sc[nm] = 0
                self.ccpool.append(nm)
            self.cci = 0
            self.PS = [ctx.enter_context(nc.psum_tensor(f"pb{i}", [128, 512], F32))
                       for i in range(8)]
            self.preg = ctx.enter_context(nc.sync.register("preg"))
            self.wslots = [{"ap": self.WB[:, i * 1024:(i + 1) * 1024],
                            "loaded": None, "free": []} for i in range(2)]
            self.wi = 0
            ncol = c["ncol"]
            self.ROWS = self.SM[:, ncol: ncol + 2 * T]
            self.NACC = self.SM[:, ncol + 2 * T: ncol + 2 * T + 2 * NT]
            self.MISC = self.SM[:, ncol + 2 * T + 2 * NT:
                                ncol + 2 * T + 2 * NT + 64]

            self.emit_all()

            block = ctx.enter_context(nc.Block())
            self.realize(block)
        return nc

    def emit_all(self):
        c = self.cfg
        T, D, L, DT, NST = c["T"], c["D"], c["L"], c["DT"], c["NST"]

        tz1 = self.op("vector", lambda e: e.memset(self.TR[:], 0.0), inc="vp")
        tz2 = self.op("vector", lambda e: e.memset(
            self.SM[:, c["ncol"]:], 0.0), inc="vp")
        self.t_zero = [tz1, tz2]
        self.t_cols = self.dma("sync", self.SM[:, 0:c["ncol"]], self.d["colsimg"])
        self.onescol = self.MISC[:, 0:1]
        self.t_ones = self.op("vector", lambda e: e.memset(self.onescol, 1.0), inc="vp")
        tz = self.op("vector", lambda e: e.memset(self.MISC[:, 1:3], 0.0), inc="vp")
        te1 = self.op("vector", lambda e: e.memset(self.MISC[0:64, 1:2], 1.0),
                      waits=[tz], inc="vp")
        self.t_e2 = self.op("vector", lambda e: e.memset(self.MISC[64:128, 2:3], 1.0),
                            waits=[te1], inc="vp")
        self.E2 = self.MISC[:, 1:3]

        t49 = self.op("vector", lambda e: e.memset(self.MISC[:, 49:50], 1.0), inc="vp")
        t50 = self.op("vector", lambda e: e.memset(self.MISC[:, 50:51], 0.0), inc="vp")
        t51 = self.op("vector", lambda e: e.memset(self.MISC[:, 51:52], 1.0), inc="vp")
        self.E01 = self.MISC[:, 49:51]
        self.E10 = self.MISC[:, 50:52]
        self.t_e01 = t51
        self.negones = self.MISC[:, 60:61]
        self.t_negones = self.op("vector", lambda e: e.memset(self.negones, -1.0),
                                 inc="vp")
        self.epscol = self.MISC[:, 48:49]
        self.t_eps = self.op("vector", lambda e: e.memset(self.epscol, self.cfg["EPS"]),
                             inc="vp")

        def loadreg(e):
            e.load(self.preg, self.d["pinfo"][0:1, 0:1])
            self.hold["pidx"] = e.snap(self.preg, min_val=0, max_val=1)
            return e.nop()
        self.op("sync", loadreg)

        xin_toks = []
        for o in range(DT):
            xin_toks.append(self.dma("sync", self.XA[:, o * T:(o + 1) * T],
                                     self.d["x_in"][o * 128:(o + 1) * 128, :]))
        self.t_mask = self.dma("sync", self.MISC[0:NST, 3:4], self.d["maskcol"])

        self.acol = {}
        for l in range(L):
            dst = self.MISC[:, 8 + l: 9 + l]
            t1 = self.op("scalar", lambda e, s=self.col(f"Alog{l}"), d=dst:
                         e.activation(out=d, in_=s, func=AF.Exp),
                         waits=[self.t_cols], inc="sp")
            t2 = self.op("scalar", lambda e, d=dst:
                         e.mul(out=d, in_=d, mul=-1.0 / D), waits=[t1], inc="sp")
            self.acol[l] = (dst, t2)

        cur = self.emit_ln(self.XA, xin_toks, self.XA, None, None)
        if self.stage == "ln":
            return self.emit_output(cur)

        if self.stage == "ffnonly":
            cur = self.emit_ffn(0, cur)
            return self.emit_output(cur)
        nl = L if self.stage in ("full",) else 1
        for l in range(nl):
            cur = self.emit_mamba(l, cur)
            if self.stage == "mamba" and l == 0:
                return self.emit_output(cur)
            cur = self.emit_attention(l, cur)
            if self.stage == "attn" and l == 0:
                return self.emit_output(cur)
            cur = self.emit_ffn(l, cur)
            if self.stage == "ffn" and l == 0:
                return self.emit_output(cur)
        cur = self.emit_tail(cur)
        return self.emit_output(cur)

    def emit_output(self, tok):
        c = self.cfg
        T, DT = c["T"], c["DT"]
        if isinstance(tok, dict):
            tok = tok["last"]
        toks = []
        for o in range(DT):
            toks.append(self.dma("scalar", self.y[o * 128:(o + 1) * 128, :],
                                 self.XA[:, o * T:(o + 1) * T], waits=[tok]))
        self.op("sync", lambda e: e.nop(), waits=toks)
        return toks[-1]

    def tile_waitall(self, tok_or_map, tc, n_o):
        if not isinstance(tok_or_map, dict):
            return tok_or_map
        best = None
        for o in range(n_o):
            t = tok_or_map.get((o, tc))
            if t is not None and (best is None or
                                  (t[0] == best[0] and t[1] > best[1])):
                best = t
        return best

    @staticmethod
    def tmap_get(tok_or_map, o, tc):
        if isinstance(tok_or_map, dict):
            return tok_or_map.get((o, tc))
        return tok_or_map

    def emit_ln(self, src, src_tok, dst, gname, bname, bc_waits=()):
        c = self.cfg
        T, D, DT, TC, NT, EPS = c["T"], c["D"], c["DT"], c["TC"], c["NT"], c["EPS"]
        ev = []
        sq_busy = [None, None]
        for tc in range(TC):
            tokA = None
            for o in range(DT):
                stile = src[:, o * T + tc * NT: o * T + tc * NT + NT]
                w = [self.tmap_get(src_tok, o, tc)]
                if o == 0 and tc == 0:
                    w += list(bc_waits)
                if o == 0:
                    w += [self.bank_free[6], self.t_ones]
                tokA = self.mm(self.PS[6][0:1, 0:NT], self.onescol, stile,
                               o == 0, o == DT - 1, waits=w,
                               inc="tp" if o == DT - 1 else None)
            tokB = None
            for o in range(DT):
                stile = src[:, o * T + tc * NT: o * T + tc * NT + NT]
                sq = self.NACC[:, (o % 2) * NT: (o % 2) * NT + NT]
                w = [self.tmap_get(src_tok, o, tc), sq_busy[o % 2]]
                if o == 0 and tc == 0:
                    w += list(bc_waits)
                tq = self.op("scalar", lambda e, s=stile, q=sq: e.activation(
                    out=q, in_=s, func=AF.Square), waits=w, inc="sp")
                w2 = [tq] + ([self.bank_free[7]] if o == 0 else [])
                tokB = self.mm(self.PS[7][0:1, 0:NT], self.onescol, sq,
                               o == 0, o == DT - 1, waits=w2, inc="tp")
                sq_busy[o % 2] = tokB
            ev.append(self.evac(6, lambda e, tc=tc: e.copy(
                out=self.row(0, slice(tc * NT, tc * NT + NT)),
                in_=self.PS[6][0:1, 0:NT]), waits=[tokA]))
            ev.append(self.evac(7, lambda e, tc=tc: e.copy(
                out=self.row(1, slice(tc * NT, tc * NT + NT)),
                in_=self.PS[7][0:1, 0:NT]), waits=[tokB]))
        s0 = self.row(0)
        s1 = self.row(1)
        t1 = self.op("scalar", lambda e: e.mul(out=s0, in_=s0, mul=1.0 / D),
                     waits=ev, inc="sp")
        t2 = self.op("scalar", lambda e: e.mul(out=s1, in_=s1, mul=1.0 / D),
                     waits=ev, inc="sp")
        tv = [t1, t2] + sq_busy
        for tc in range(TC):
            tsl = slice(tc * NT, tc * NT + NT)
            s2 = self.NACC[0:1, 0:NT]
            ta_ = self.op("vector", lambda e, s2=s2, tsl=tsl: e.tensor_tensor(
                out=s2, in0=s0[:, tsl], in1=s0[:, tsl], op=ALU.mult),
                waits=tv, inc="vp")
            tv = [self.op("vector", lambda e, s2=s2, tsl=tsl: e.tensor_tensor(
                out=s1[:, tsl], in0=s1[:, tsl], in1=s2, op=ALU.subtract),
                waits=[ta_], inc="vp")]
        t5 = self.op("scalar", lambda e: e.activation(
            out=s1, in_=s1, func=AF.Ln, bias=self.epscol[0:1, :]),
            waits=tv, inc="sp")
        t6 = self.op("scalar", lambda e: e.activation(
            out=s1, in_=s1, func=AF.Exp, scale=-0.5), waits=[t5], inc="sp")
        t7 = self.op("vector", lambda e: e.tensor_tensor(
            out=s0, in0=s0, in1=s1, op=ALU.mult), waits=[t6], inc="vp")
        t8 = self.op("vector", lambda e: e.tensor_scalar(
            out=s0, in0=s0, scalar1=-1.0, scalar2=None, op0=ALU.mult),
            waits=[t7], inc="vp")
        out_tok = None
        tmap = {}
        slot_busy = []
        for tc in range(TC):
            c1b = self.NACC[:, 0:NT]
            c0b = self.NACC[:, NT:2 * NT]
            tsl = slice(tc * NT, tc * NT + NT)
            tb1 = self.bcast(s1[:, tsl], c1b, 128, waits=[t8] + slot_busy)
            tb0 = self.bcast(s0[:, tsl], c0b, 128, waits=[tb1])
            last_ta = None
            for o in range(DT):
                sl = slice(o * T + tc * NT, o * T + tc * NT + NT)
                stile, dtile = src[:, sl], dst[:, sl]
                tm = self.op("vector", lambda e, s=stile, d=dtile, c1b=c1b:
                             e.tensor_tensor(out=d, in0=s, in1=c1b, op=ALU.mult),
                             waits=[tb1, tb0], inc="vp")
                ta = self.op("vector", lambda e, d=dtile, c0b=c0b:
                             e.tensor_tensor(out=d, in0=d, in1=c0b, op=ALU.add),
                             waits=[tm], inc="vp")
                if gname is not None:
                    out_tok = self.op("scalar", lambda e, d=dtile, o=o: e.activation(
                        out=d, in_=d, func=AF.Identity, bias=self.col(bname, o),
                        scale=self.col(gname, o)), waits=[ta, self.t_cols], inc="sp")
                else:
                    out_tok = ta
                last_ta = ta
                tmap[(o, tc)] = out_tok
            slot_busy = [last_ta]
        tmap["last"] = out_tok
        return tmap

    def emit_mamba(self, l, x0_tok):
        c = self.cfg
        T, D, NT, TC, NST = c["T"], c["D"], c["NT"], c["TC"], c["NST"]
        DT, DIT = c["DT"], c["DIT"]
        MNT = min(256, T)
        MC = T // MNT
        xi_cols = DIT * MNT
        AU = self.trr(xi_cols, 2 * T)
        zoff = xi_cols + 2 * T
        z_busy = [None, None]
        xi_busy = None
        m_ev, u_ev, mo_ev = [], [], []
        last_wout = None
        nbd_tok = []
        for o in range(DT):
            nbd_tok.append(self.op(
                "scalar", lambda e, o=o: e.mul(out=self.MISC[:, 52 + o: 53 + o],
                                               in_=self.col(f"b_dt{l}", o), mul=-1.0),
                waits=[self.t_cols], inc="sp"))

        for mc in range(MC):
            xi = [self.trr(dc * MNT, MNT) for dc in range(DIT)]
            xi_toks = [None] * DIT
            xa_sl = lambda dc: self.XA[:, dc * T + mc * MNT: dc * T + mc * MNT + MNT]
            for os_ in range(DIT):
                slab = self.load_slab(self.d[f"wi{l}"][os_ * 128:(os_ + 1) * 128, :], D)
                bi = os_ % 2
                tokm = None
                for dc in range(DT):
                    w = ()
                    if dc == 0:
                        w = [self.tile_waitall(x0_tok, (mc * MNT) // NT, DT),
                             slab["loaded"], self.bank_free[bi], xi_busy]
                    tokm = self.mm(self.PS[bi][:, 0:MNT],
                                   slab["ap"][:, dc * 128:(dc + 1) * 128],
                                   xa_sl(dc), dc == 0, dc == DT - 1, waits=w,
                                   inc="tp" if dc == DT - 1 else None)
                slab["free"].append(tokm)
                xi_toks[os_] = self.evac(
                    bi, lambda e, q=xi[os_], bi=bi, os_=os_: e.activation(
                        out=q, in_=self.PS[bi][:, 0:MNT], func=AF.Silu,
                        bias=self.col(f"b_in{l}", os_)),
                    waits=[tokm, self.t_cols])
            tok4 = None
            pendm = None

            def emit_mfold(po, ptsp2, psp_tile):
                w4 = [ptsp2, self.t_negones] + \
                    ([self.bank_free[4]] if po == 0 else [])
                t4 = self.mm(self.PS[4][0:1, 0:MNT], self.negones, psp_tile,
                             po == 0, po == DT - 1, waits=w4,
                             inc="tp" if po == DT - 1 else None)
                z_busy[po % 2] = t4
                return t4

            for o in range(DT):
                srcw = self.d[f"wdt{l}"][o * 128:(o + 1) * 128, :]
                npc = (DIT * 128 + 1023) // 1024
                pieces = []
                for pc in range(npc):
                    cc = min(1024, DIT * 128 - pc * 1024)
                    pieces.append(self.load_slab(srcw[:, pc * 1024: pc * 1024 + cc], cc))
                bi = 2 + o % 2
                tokm = None
                for dc in range(DIT):
                    pc, off = divmod(dc * 128, 1024)
                    w = []
                    if off == 0:
                        w.append(pieces[pc]["loaded"])
                    if dc == 0:
                        w += [self.bank_free[bi]]
                    if o == 0:
                        w.append(xi_toks[dc])
                    tokm = self.mm(self.PS[bi][:, 0:MNT],
                                   pieces[pc]["ap"][:, off: off + 128],
                                   xi[dc], dc == 0, dc == DIT - 1, waits=w,
                                   inc="tp" if dc == DIT - 1 else None)
                for pcs in pieces:
                    pcs["free"].append(tokm)
                sp_tile = self.trr(zoff + (o % 2) * MNT, MNT)
                tsp = self.op("scalar", lambda e, q=sp_tile, bi=bi, o=o: e.activation(
                    out=q, in_=self.PS[bi][:, 0:MNT], func=AF.Sigmoid, scale=-1.0,
                    bias=self.MISC[:, 52 + o: 53 + o]),
                    waits=[tokm, nbd_tok[o], z_busy[o % 2]], inc="sp")
                self.bank_free[bi] = tsp
                tsp2 = self.op("scalar", lambda e, q=sp_tile: e.activation(
                    out=q, in_=q, func=AF.Ln), waits=[tsp], inc="sp")
                if pendm is not None:
                    emit_mfold(*pendm)
                pendm = (o, tsp2, sp_tile)
            tok4 = emit_mfold(*pendm)
            m_ev.append(self.evac(4, lambda e, mc=mc: e.copy(
                out=self.row(0, slice(mc * MNT, mc * MNT + MNT)),
                in_=self.PS[4][0:1, 0:MNT]), waits=[tok4]))
            slab = self.load_slab(self.d[f"wxp{l}"], DIT * NST)
            tokm = None
            for dc in range(DIT):
                w = ([slab["loaded"], xi_toks[-1], self.bank_free[5]]
                     if dc == 0 else ())
                tokm = self.mm(self.PS[5][0:NST, 0:MNT],
                               slab["ap"][0:128, dc * NST:(dc + 1) * NST],
                               xi[dc], dc == 0, dc == DIT - 1, waits=w,
                               inc="tp" if dc == DIT - 1 else None)
            slab["free"].append(tokm)
            u_ev.append(self.evac(5, lambda e, mc=mc: e.activation(
                out=AU[0:NST, T + mc * MNT: T + mc * MNT + MNT],
                in_=self.PS[5][0:NST, 0:MNT], func=AF.Identity,
                bias=self.col(f"b_xp{l}")[0:NST, :]),
                waits=[tokm, self.t_cols]))
            pendo = None

            def emit_wout(pos, ptz, pz, pslab2):
                nonlocal last_wout
                for o in range(DT):
                    bi2 = 4 + o // 2
                    colh = (o % 2) * MNT
                    st = (pos == 0 and o % 2 == 0)
                    w = [ptz, pslab2["loaded"]]
                    if st:
                        w.append(self.bank_free[bi2])
                    stopf = (pos == DIT - 1 and o % 2 == 1)
                    last_wout = self.mm(
                        self.PS[bi2][:, colh: colh + MNT],
                        pslab2["ap"][:, o * 128:(o + 1) * 128], pz,
                        st, stopf, waits=w, inc="tp")
                pslab2["free"].append(last_wout)
                z_busy[pos % 2] = last_wout

            for os_ in range(DIT):
                slab = self.load_slab(
                    self.d[f"wi{l}"][(DIT + os_) * 128:(DIT + os_ + 1) * 128, :], D)
                bi = os_ % 2
                tokm = None
                for dc in range(DT):
                    w = ()
                    if dc == 0:
                        w = [self.tile_waitall(x0_tok, (mc * MNT) // NT, DT),
                             slab["loaded"], self.bank_free[bi]]
                    tokm = self.mm(self.PS[bi][:, 0:MNT],
                                   slab["ap"][:, dc * 128:(dc + 1) * 128],
                                   xa_sl(dc), dc == 0, dc == DT - 1, waits=w,
                                   inc="tp" if dc == DT - 1 else None)
                slab["free"].append(tokm)
                z = self.trr(zoff + (os_ % 2) * MNT, MNT)
                tsil = self.op("scalar", lambda e, z=z, bi=bi, os_=os_: e.activation(
                    out=z, in_=self.PS[bi][:, 0:MNT], func=AF.Silu,
                    bias=self.col(f"b_in{l}", DIT + os_)),
                    waits=[tokm, self.t_cols, z_busy[os_ % 2]], inc="sp")
                self.bank_free[bi] = tsil
                tz = self.op("vector", lambda e, z=z, q=xi[os_]: e.tensor_tensor(
                    out=z, in0=z, in1=q, op=ALU.mult),
                    waits=[tsil, xi_toks[os_]], inc="vp")
                if pendo is not None:
                    emit_wout(*pendo)
                slab2 = self.load_slab(
                    self.d[f"wout{l}"][os_ * 128:(os_ + 1) * 128, :], D)
                pendo = (os_, tz, z, slab2)
            emit_wout(*pendo)
            for o in range(DT):
                bi2 = 4 + o // 2
                colh = (o % 2) * MNT
                mo_ev.append(self.evac(bi2, lambda e, o=o, bi2=bi2, colh=colh, mc=mc:
                                       e.copy(out=self.XB[:, o * T + mc * MNT:
                                                          o * T + mc * MNT + MNT],
                                              in_=self.PS[bi2][:, colh: colh + MNT]),
                                       waits=[last_wout]))
            xi_busy = last_wout

        SCOUT = AU[64:128, 0:T]
        SCOUT2 = AU[64:128, T:2 * T]
        tb = self.bcast(self.row(0), AU[0:64, 0:T], 64, waits=[m_ev[-1]])
        ta = self.op("scalar", lambda e: e.activation(
            out=AU[0:64, 0:T], in_=AU[0:64, 0:T], func=AF.Exp,
            scale=self.acol[l][0][0:64, :]),
            waits=[tb, self.acol[l][1]], inc="sp")
        t_s1 = self.op("vector", lambda e: e.tensor_tensor_scan(
            out=SCOUT[0:NST, :], data0=AU[0:NST, 0:T], data1=AU[0:NST, T:2 * T],
            initial=0.0, op0=ALU.mult, op1=ALU.add),
            waits=[ta, u_ev], inc="vp")
        t_cr = self.dma("scalar", self.cr_loc.ap(), SCOUT[0:NST, T - 1: T],
                        waits=[t_s1])
        ccs = self.ccpool[self.cci % len(self.ccpool)]; self.cci += 1
        t_ag = self.op("gpsimd", lambda e: e.collective_compute(
            "AllGather", ALU.bypass, replica_groups=RG2,
            ins=[self.cr_loc.ap().opt()], outs=[self.cr_all.ap().opt()]),
            waits=[t_cr], inc=ccs)
        t_cl = self.dma("scalar", self.MISC[0:NST, 4:5], self.cr_all.ap()[0],
                        waits=[t_ag])
        t_init = self.op("vector", lambda e: e.tensor_tensor(
            out=self.MISC[0:NST, 5:6], in0=self.MISC[0:NST, 4:5],
            in1=self.MISC[0:NST, 3:4], op=ALU.mult),
            waits=[t_cl, self.t_mask], inc="vp")
        t_s2 = self.op("vector", lambda e: e.tensor_tensor_scan(
            out=SCOUT2[0:NST, :], data0=AU[0:NST, 0:T], data1=AU[0:NST, T:2 * T],
            initial=self.MISC[0:NST, 5:6], op0=ALU.mult, op1=ALU.add),
            waits=[t_init], inc="vp")
        ss_ev = []
        for tc in range(TC):
            w = [t_s2, self.bank_free[0], self.t_ones]
            tokm = self.mm(self.PS[0][0:1, 0:NT], self.onescol[64:128, :],
                           SCOUT2[0:NST, tc * NT: tc * NT + NT],
                           True, True, waits=w, inc="tp", tp=(64, 0))
            ss_ev.append(self.evac(0, lambda e, tc=tc: e.copy(
                out=self.row(1, slice(tc * NT, tc * NT + NT)),
                in_=self.PS[0][0:1, 0:NT]), waits=[tokm]))

        t_last = None
        smap = {}
        sb_busy = []
        for tc in range(TC):
            sb = self.trr(zoff, NT)
            tb = self.bcast(self.row(1, slice(tc * NT, tc * NT + NT)),
                            sb, 128,
                            waits=[ss_ev[tc], z_busy[0], z_busy[1]] + sb_busy)
            for o in range(DT):
                sl = slice(o * T + tc * NT, o * T + tc * NT + NT)
                mo = self.XB[:, sl]
                xa = self.XA[:, sl]
                t1 = self.op("vector", lambda e, mo=mo, sb=sb: e.tensor_tensor(
                    out=mo, in0=mo, in1=sb, op=ALU.mult),
                    waits=[tb, mo_ev], inc="vp")
                t2 = self.op("vector", lambda e, xa=xa, o=o: e.tensor_scalar(
                    out=xa, in0=xa, scalar1=self.col(f"Dp1{l}", o),
                    scalar2=self.col(f"b_out{l}", o), op0=ALU.mult, op1=ALU.add),
                    waits=[self.t_cols, t1], inc="vp")
                t_last = self.op("vector", lambda e, xa=xa, mo=mo: e.tensor_tensor(
                    out=xa, in0=xa, in1=mo, op=ALU.add),
                    waits=[t2], inc="vp")
                smap[(o, tc)] = t_last
                sb_busy = [t1]
        return self.emit_ln(self.XA, smap, self.XA, f"g1{l}", f"be1{l}")

    def emit_attention(self, l, x1_tok):
        c = self.cfg
        T, D, NT, TC, NP, KC, QS = (c["T"], c["D"], c["NT"], c["TC"], c["NP"],
                                    c["KC"], c["QS"])
        DT, NST, SCALE = c["DT"], c["NST"], c["SCALE"]
        QOFF = 4 * T
        PTOFF = 5 * T

        kv_dma = []
        for f in range(NP):
            slab = self.load_slab(self.d[f"wk{l}"][f * 128:(f + 1) * 128, :], D)
            for tc in range(TC):
                bi = tc % 2
                tokm = None
                for dc in range(DT):
                    w = ()
                    if dc == 0:
                        w = [self.tile_waitall(x1_tok, tc, DT),
                             slab["loaded"], self.bank_free[bi]]
                    tokm = self.mm(self.PS[bi][:, 0:NT],
                                   slab["ap"][:, dc * 128:(dc + 1) * 128],
                                   self.XA[:, dc * T + tc * NT: dc * T + tc * NT + NT],
                                   dc == 0, dc == DT - 1, waits=w,
                                   inc="tp" if dc == DT - 1 else None)
                ek = self.evac(bi, lambda e, f=f, tc=tc, bi=bi: e.activation(
                    out=self.XB[:, f * T + tc * NT: f * T + tc * NT + NT],
                    in_=self.PS[bi][:, 0:NT], func=AF.Identity, bias=self.col(f"b_k{l}", f)),
                    waits=[tokm, self.t_cols])
            slab["free"].append(tokm)
            kv_dma.append(self.dma("scalar", self.kv_loc.ap()[f * 128:(f + 1) * 128, :],
                                   self.XB[:, f * T:(f + 1) * T], waits=[ek]))
        bvb = self.trr(QOFF, D)
        t_bvb = self.bcast(self.d["bvrows"][l: l + 1, :], bvb, 128, waits=())
        v_ev = []
        kdma_all = list(kv_dma)
        for pp in range(NP):
            slab = self.load_slab(self.d[f"wv{l}"][pp * 128:(pp + 1) * 128, :], D)
            G4 = min(4, KC)
            for ttg in range(KC // G4):
                bi = ttg % 2
                vfirst = True
                tokm = None
                for dc in range(DT):
                    for j in range(G4):
                        tt = ttg * G4 + j
                        w = ()
                        if vfirst:
                            w = [slab["loaded"], self.bank_free[bi],
                                 self.tile_waitall(
                                     x1_tok, (ttg * G4 * 128) // NT, DT),
                                 self.tile_waitall(
                                     x1_tok, ((ttg + 1) * G4 * 128 - 1) // NT, DT)]
                        stopf = (dc == DT - 1 and j == G4 - 1)
                        tokm = self.mm(
                            self.PS[bi][:, j * 128:(j + 1) * 128],
                            self.XA[:, dc * T + tt * 128: dc * T + tt * 128 + 128],
                            slab["ap"][:, dc * 128:(dc + 1) * 128],
                            vfirst, stopf, waits=w, inc="tp")
                        vfirst = False
                ev_last = None
                for j in range(G4):
                    tt = ttg * G4 + j
                    ev_last = self.op(
                        "vector", lambda e, j=j, tt=tt, bi=bi, pp=pp: e.tensor_tensor(
                            out=self.XB[:, pp * T + tt * 128: pp * T + tt * 128 + 128],
                            in0=self.PS[bi][:, j * 128:(j + 1) * 128],
                            in1=bvb[:, pp * 128:(pp + 1) * 128],
                            op=ALU.add),
                        waits=[tokm, t_bvb, kdma_all], inc="vp")
                self.bank_free[bi] = ev_last
                v_ev.append(ev_last)
            slab["free"].append(tokm)
        for p in range(NP):
            kv_dma.append(self.dma(
                "scalar", self.kv_loc.ap()[D + p * 128: D + (p + 1) * 128, :],
                self.XB[:, p * T:(p + 1) * T], waits=[v_ev[-1]]))
        ccs = self.ccpool[self.cci % len(self.ccpool)]; self.cci += 1
        t_agkv = self.op("gpsimd", lambda e: e.collective_compute(
            "AllGather", ALU.bypass, replica_groups=RG2,
            ins=[self.kv_loc.ap().opt()], outs=[self.kv_all.ap().opt()]),
            waits=kv_dma, inc=ccs)

        kslot = [{"off": 0, "busy": None}, {"off": 2 * T, "busy": None}]
        vslot = [{"off": T, "busy": None}, {"off": 3 * T, "busy": None}]
        ptbusy = [None] * 4
        nacc_busy = []
        rb_busy = []
        qbusy = None
        xb_part_tok = {}
        nrow_tok = {}
        ofinal_map = {}
        q_spill_tok = [None] * NP
        si = 0
        ofinal = []
        for half in range(2):
            for hp in range(NP):
                if half == 0:
                    slab = self.load_slab(
                        self.d[f"wq{l}"][hp * 128:(hp + 1) * 128, :], D)
                    Q = self.trr(QOFF, T)
                    eq = None
                    for tc in range(TC):
                        bi = tc % 2
                        tokm = None
                        for dc in range(DT):
                            w = ()
                            if dc == 0:
                                w = [self.tile_waitall(x1_tok, tc, DT),
                                     slab["loaded"], self.bank_free[bi], qbusy,
                                     t_bvb]
                            tokm = self.mm(
                                self.PS[bi][:, 0:NT],
                                slab["ap"][:, dc * 128:(dc + 1) * 128],
                                self.XA[:, dc * T + tc * NT: dc * T + tc * NT + NT],
                                dc == 0, dc == DT - 1, waits=w,
                                inc="tp" if dc == DT - 1 else None)
                        eq = self.evac(bi, lambda e, tc=tc, bi=bi, hp=hp: e.activation(
                            out=self.trr(QOFF + tc * NT, NT), in_=self.PS[bi][:, 0:NT],
                            func=AF.Identity, bias=self.col(f"b_q{l}", hp)),
                            waits=[tokm, self.t_cols,
                                   q_spill_tok[hp - 1] if hp > 0 else None])
                    slab["free"].append(tokm)
                    q_spill_tok[hp] = self.dma(
                        "scalar", self.q_spill.ap()[hp * 128:(hp + 1) * 128, :],
                        Q, waits=[eq])
                    q_tok = eq
                else:
                    Q = self.trr(QOFF, T)
                    q_tok = self.dma(
                        "sync", Q, self.q_spill.ap()[hp * 128:(hp + 1) * 128, :],
                        waits=[q_spill_tok, qbusy])
                ks = kslot[si % 2]
                vs = vslot[si % 2]
                si += 1
                Kh = self.trr(ks["off"], T)
                Vh = self.trr(vs["off"], T)
                if half == 0:
                    ksrc = self.kv_loc.ap()[hp * 128:(hp + 1) * 128, :]
                    vsrc = self.kv_loc.ap()[D + hp * 128: D + (hp + 1) * 128, :]
                    kw = [kv_dma[hp], ks["busy"]]
                    vw = [kv_dma[NP + hp], vs["busy"]]
                else:
                    ksrc = None
                    vsrc = None
                    kw = [t_agkv, ks["busy"]]
                    vw = [t_agkv, vs["busy"]]
                if half == 0:
                    tk = self.dma("sync", Kh, ksrc, waits=kw)
                    tv = self.dma("sync", Vh, vsrc, waits=vw)
                else:
                    ksem, vsem = self.dsem(), self.dsem()
                    kprev, vprev = self.dlast.get(ksem), self.dlast.get(vsem)
                    tk = self.op("sync", lambda e, Kh=Kh, hp=hp: e.dma_start(
                        out=Kh, in_=self.kv_all.ap()[self.hold["pidx"]]
                        [hp * 128:(hp + 1) * 128, :]),
                        waits=self._flat(kw) + ((kprev,) if kprev else ()),
                        inc=ksem, amt=16, is_dma=True)
                    tv = self.op("sync", lambda e, Vh=Vh, hp=hp: e.dma_start(
                        out=Vh, in_=self.kv_all.ap()[self.hold["pidx"]]
                        [D + hp * 128: D + (hp + 1) * 128, :]),
                        waits=self._flat(vw) + ((vprev,) if vprev else ()),
                        inc=vsem, amt=16, is_dma=True)
                    self.dlast[ksem] = tk
                    self.dlast[vsem] = tv
                if half == 0:
                    tqb = self.emit_bounds(l, hp, Q, q_tok, Kh, tk, PTOFF)
                else:
                    tqb = None
                for qs in range(QS):
                    ob = 2 + qs % 2
                    pend = None

                    def emit_ctx(pk, pte1, pte2, ppt1, ppt2, ptn1, ptn2,
                                 ob=ob, tv=tv, Vh=Vh):
                        wv1 = [pte1, tv]
                        if pk == 0:
                            wv1.append(self.bank_free[ob])
                        c1 = self.mm(self.PS[ob][0:64, 0:NT],
                                     Vh[:, pk * 128: pk * 128 + 64], ppt1,
                                     pk == 0, pk == KC - 1, waits=wv1,
                                     inc="tp", tp=(0, 0), skip=True)
                        c2 = self.mm(self.PS[ob][64:128, 0:NT],
                                     Vh[:, pk * 128 + 64: pk * 128 + 128],
                                     ppt2, pk == 0, pk == KC - 1,
                                     waits=[pte2, tv] + ([self.bank_free[ob]]
                                                         if pk == 0 else []),
                                     inc="tp", tp=(0, 64), skip=True)
                        ptbusy[pk % 2] = [ptn1, c1]
                        ptbusy[2 + pk % 2] = [ptn2, c2]
                        return c1, c2

                    for kc in range(KC):
                        sb1 = (0, 1, 5)[kc % 3]
                        sb2 = (6, 7, 4)[kc % 3]
                        w1 = [tk, q_tok, self.bank_free[sb1]]
                        tm1 = self.mm(self.PS[sb1][:, 0:NT],
                                      Kh[0:64, kc * 128:(kc + 1) * 128],
                                      Q[0:64, qs * NT: qs * NT + NT],
                                      True, True, waits=w1, inc="tp", tp=(0, 0))
                        tm2 = self.mm(self.PS[sb2][:, 0:NT],
                                      Kh[64:128, kc * 128:(kc + 1) * 128],
                                      Q[64:128, qs * NT: qs * NT + NT],
                                      True, True,
                                      waits=[tk, q_tok, self.bank_free[sb2]],
                                      inc="tp", tp=(64, 0))
                        pt1 = self.trr(PTOFF + (kc % 2) * NT, NT)
                        pt2 = self.trr(PTOFF + (2 + kc % 2) * NT, NT)
                        te1 = self.op("scalar", lambda e, pt=pt1, b=sb1, hp=hp:
                                      e.activation(out=pt, in_=self.PS[b][:, 0:NT],
                                                   func=AF.Exp, scale=SCALE,
                                                   bias=self.MISC[:, 16 + 2 * hp:
                                                                  17 + 2 * hp]),
                                      waits=[tm1, tqb] + (ptbusy[kc % 2] or []),
                                      inc="sp")
                        self.bank_free[sb1] = te1
                        te2 = self.op("scalar", lambda e, pt=pt2, b=sb2, hp=hp:
                                      e.activation(out=pt, in_=self.PS[b][:, 0:NT],
                                                   func=AF.Exp, scale=SCALE,
                                                   bias=self.MISC[:, 17 + 2 * hp:
                                                                  18 + 2 * hp]),
                                      waits=[tm2, tqb] + (ptbusy[2 + kc % 2] or []),
                                      inc="sp")
                        self.bank_free[sb2] = te2
                        if kc == 0:
                            tn1 = self.op("vector", lambda e, pt=pt1: e.tensor_copy(
                                self.NACC[:, 0:NT], pt),
                                waits=[te1] + nacc_busy, inc="vp")
                            tn2 = self.op("vector", lambda e, pt=pt2: e.tensor_copy(
                                self.NACC[:, NT:2 * NT], pt), waits=[te2], inc="vp")
                            nacc_busy = []
                        else:
                            tn1 = self.op("vector", lambda e, pt=pt1: e.tensor_tensor(
                                out=self.NACC[:, 0:NT], in0=self.NACC[:, 0:NT],
                                in1=pt, op=ALU.add), waits=[te1], inc="vp")
                            tn2 = self.op("vector", lambda e, pt=pt2: e.tensor_tensor(
                                out=self.NACC[:, NT:2 * NT], in0=self.NACC[:, NT:2 * NT],
                                in1=pt, op=ALU.add), waits=[te2], inc="vp")
                        if pend is not None:
                            tc1, tc2 = emit_ctx(*pend)
                        pend = (kc, te1, te2, pt1, pt2, tn1, tn2)
                    tc1, tc2 = emit_ctx(*pend)
                    tf1 = self.mm(self.PS[4][0:2, 0:NT], self.E01,
                                  self.NACC[:, 0:NT], True, False,
                                  waits=[tn1, self.bank_free[4], self.t_e01] + nacc_busy,
                                  inc="tp")
                    tf2 = self.mm(self.PS[4][0:2, 0:NT], self.E10,
                                  self.NACC[:, NT:2 * NT], False, True,
                                  waits=[tn2], inc="tp")
                    qsl = slice(qs * NT, qs * NT + NT)
                    stage = self.NACC[0:2, NT:2 * NT]
                    if half == 0:
                        es = self.evac(4, lambda e, stage=stage: e.tensor_copy(
                            stage, self.PS[4][0:2, 0:NT]),
                            waits=[tf2, tn2], eng="vector")
                        tspill = self.dma(
                            "scalar",
                            self.n_spill.ap()[2 * hp: 2 * hp + 2, qsl], stage,
                            waits=[es])
                        nrow_tok[(hp, qs)] = tspill
                        nacc_busy = [es, tspill]
                        ot = self.evac(ob, lambda e, hp=hp, qs=qs, ob=ob:
                                       e.tensor_copy(
                                           self.XB[:, hp * T + qs * NT:
                                                   hp * T + qs * NT + NT],
                                           self.PS[ob][:, 0:NT]),
                                       waits=[tc1, tc2, v_ev[-1]], eng="vector")
                        xb_part_tok[(hp, qs)] = ot
                    else:
                        tload = self.dma(
                            "sync", stage,
                            self.n_spill.ap()[2 * hp: 2 * hp + 2, qsl],
                            waits=[nrow_tok[(hp, qs)], tf2] + nacc_busy)
                        rec = self.NACC[0:2, 0:NT]
                        tr_ = self.op("vector", lambda e, rec=rec, stage=stage:
                                      e.tensor_tensor(out=rec, in0=self.PS[4][0:2, 0:NT],
                                                      in1=stage, op=ALU.add),
                                      waits=[tf2, tload, tn1], inc="vp")
                        self.bank_free[4] = tr_
                        tr2 = self.op("vector", lambda e, rec=rec:
                                      e.reciprocal(out=rec, in_=rec),
                                      waits=[tr_], inc="vp")
                        rb = self.trr(PTOFF, NT)
                        tb1 = self.bcast(rec[0:1, :], rb[0:64, :], 64,
                                         waits=[tr2] + (ptbusy[0] or []) +
                                         (ptbusy[1] or []) + rb_busy)
                        tb2 = self.bcast(rec[1:2, :], rb[64:128, :], 64,
                                         waits=[tr2])
                        nacc_busy = [tr2, tb1, tb2]
                        t1 = self.op("vector", lambda e, hp=hp, qs=qs, ob=ob:
                                     e.tensor_tensor(
                                         out=self.XB[:, hp * T + qs * NT:
                                                     hp * T + qs * NT + NT],
                                         in0=self.PS[ob][:, 0:NT],
                                         in1=self.XB[:, hp * T + qs * NT:
                                                     hp * T + qs * NT + NT],
                                         op=ALU.add),
                                     waits=[tc1, tc2, xb_part_tok[(hp, qs)]],
                                     inc="vp")
                        self.bank_free[ob] = t1
                        t2 = self.op("vector", lambda e, hp=hp, qs=qs:
                                     e.tensor_tensor(
                                         out=self.XB[:, hp * T + qs * NT:
                                                     hp * T + qs * NT + NT],
                                         in0=self.XB[:, hp * T + qs * NT:
                                                     hp * T + qs * NT + NT],
                                         in1=rb, op=ALU.mult),
                                     waits=[tb1, tb2, t1], inc="vp")
                        ofinal.append(t2)
                        ofinal_map[(hp, qs)] = t2
                        ptbusy[0] = (ptbusy[0] or []) + [t2]
                        rb_busy = [t2]
                ks["busy"] = self._engine_now("tensor")
                vs["busy"] = self._engine_now("tensor")
                qbusy = self._engine_now("tensor")

        t_last = None
        for o in range(self.cfg["DT"]):
            slab = self.load_slab(self.d[f"wao{l}"][o * 128:(o + 1) * 128, :], D)
            for tc in range(TC):
                bi = tc % 2
                tokm = None
                for dc in range(DT):
                    w = ()
                    if dc == 0:
                        w = [self.tile_waitall(ofinal_map, tc, DT),
                             slab["loaded"], self.bank_free[bi]]
                    tokm = self.mm(self.PS[bi][:, 0:NT],
                                   slab["ap"][:, dc * 128:(dc + 1) * 128],
                                   self.XB[:, dc * T + tc * NT: dc * T + tc * NT + NT],
                                   dc == 0, dc == DT - 1, waits=w,
                                   inc="tp" if dc == DT - 1 else None)
                t_last = self.evac(bi, lambda e, o=o, tc=tc, bi=bi:
                                   e.scalar_tensor_tensor(
                                       out=self.XA[:, o * T + tc * NT:
                                                   o * T + tc * NT + NT],
                                       in0=self.PS[bi][:, 0:NT],
                                       scalar=self.col(f"b_ao{l}", o),
                                       in1=self.XA[:, o * T + tc * NT:
                                                   o * T + tc * NT + NT],
                                       op0=ALU.add, op1=ALU.add),
                                   waits=[tokm, self.t_cols], eng="vector")
            slab["free"].append(tokm)
        return self.emit_ln(self.XA, t_last, self.XA, f"g2{l}", f"be2{l}")

    def _max_tok(self, a, b):
        if a is None:
            return b
        if b is None:
            return a
        if a[0] == b[0]:
            return a if a[1] >= b[1] else b
        return b

    def _engine_now(self, eng):
        sem = {"tensor": "tp", "vector": "vp", "scalar": "sp", "gpsimd": "gp"}[eng]
        return (sem, self.sc[sem])

    def emit_bounds(self, l, hp, Q, q_tok, Kh, tk, PTOFF):
        c = self.cfg
        T, NT, TC, SCALE = c["T"], c["NT"], c["TC"], c["SCALE"]
        sq = self.trr(PTOFF, NT)
        toks = []
        for (src_ap, src_tok, rbase) in ((Q, q_tok, 0), (Kh, tk, 1)):
            for tc in range(TC):
                tq = self.op("scalar", lambda e, s=src_ap, tc=tc, sq=sq:
                             e.activation(out=sq, in_=s[:, tc * NT: tc * NT + NT],
                                          func=AF.Square),
                             waits=[src_tok] + toks[-1:], inc="sp")
                tm = self.mm(self.PS[1][0:2, 0:NT], self.E2, sq, True, True,
                             waits=[tq, self.bank_free[1], self.t_e2], inc="tp")
                toks.append(self.evac(1, lambda e, rbase=rbase, tc=tc: e.copy(
                    out=self.row(rbase, slice(tc * NT, tc * NT + NT), np_=2),
                    in_=self.PS[1][0:2, 0:NT]), waits=[tm]))
        qn = self.row(0, np_=2)
        kn = self.row(1, np_=2)
        qm = self.MISC[0:2, 32 + 2 * hp: 33 + 2 * hp]
        km = self.MISC[0:2, 33 + 2 * hp: 34 + 2 * hp]
        t1 = self.op("vector", lambda e: e.reduce_max(out=qm, in_=qn,
                                                      axis=mybir.AxisListType.X),
                     waits=toks, inc="vp")
        t2 = self.op("vector", lambda e: e.reduce_max(out=km, in_=kn,
                                                      axis=mybir.AxisListType.X),
                     waits=toks, inc="vp")
        t3 = self.op("vector", lambda e: e.tensor_tensor(
            out=qm, in0=qm, in1=km, op=ALU.mult), waits=[t1, t2], inc="vp")
        t4 = self.op("scalar", lambda e: e.activation(
            out=qm, in_=qm, func=AF.Sqrt, scale=(SCALE * 1.25) ** 2),
            waits=[t3], inc="sp")
        t5 = self.op("vector", lambda e: e.tensor_scalar(
            out=qm, in0=qm, scalar1=-1.0, scalar2=-2.0,
            op0=ALU.mult, op1=ALU.add), waits=[t4], inc="vp")
        tb1 = self.bcast(qm[0:1, :], self.MISC[:, 16 + 2 * hp: 17 + 2 * hp], 128,
                         waits=[t5])
        tb2 = self.bcast(qm[1:2, :], self.MISC[:, 17 + 2 * hp: 18 + 2 * hp], 128,
                         waits=[t5])
        return self._max_tok(tb1, tb2)

    def emit_ffn(self, l, x2_tok):
        c = self.cfg
        T, D, NT, TC, DT, DFT = c["T"], c["D"], c["NT"], c["TC"], c["DT"], c["DFT"]
        t_last = None
        x3map = {}
        h_tok = {}
        for tc in range(TC):
            for os_ in range(DFT):
                slab = self.load_slab(self.d[f"wf1{l}"][os_ * 128:(os_ + 1) * 128, :], D)
                bi = os_ % 2
                tokm = None
                for dc in range(DT):
                    w = ()
                    if dc == 0:
                        w = [self.tile_waitall(x2_tok, tc, DT),
                             slab["loaded"], self.bank_free[bi]]
                    tokm = self.mm(self.PS[bi][:, 0:NT],
                                   slab["ap"][:, dc * 128:(dc + 1) * 128],
                                   self.XA[:, dc * T + tc * NT: dc * T + tc * NT + NT],
                                   dc == 0, dc == DT - 1, waits=w,
                                   inc="tp" if dc == DT - 1 else None)
                slab["free"].append(tokm)
                h_tok[os_] = self.evac(bi, lambda e, os_=os_, bi=bi: e.activation(
                    out=self.XB[:, os_ * NT: os_ * NT + NT], in_=self.PS[bi][:, 0:NT],
                    func=AF.Gelu, bias=self.col(f"b_f1{l}", os_)),
                    waits=[tokm, self.t_cols])
            for o in range(DT):
                bi = 2 + o % 2
                tokm = None
                nq4 = (DFT * 128) // 1024
                for q4 in range(nq4):
                    slab = self.load_slab(
                        self.d[f"wf2{l}"][o * 128:(o + 1) * 128,
                                          q4 * 1024:(q4 + 1) * 1024], 1024)
                    for j in range(8):
                        dc = q4 * 8 + j
                        w = [slab["loaded"], h_tok[dc]]
                        if q4 == 0 and j == 0:
                            w.append(self.bank_free[bi])
                        tokm = self.mm(self.PS[bi][:, 0:NT],
                                       slab["ap"][:, j * 128:(j + 1) * 128],
                                       self.XB[:, dc * NT: dc * NT + NT],
                                       dc == 0, dc == DFT - 1, waits=w,
                                       inc="tp" if (j == 7 or dc == DFT - 1)
                                       else None)
                    slab["free"].append(tokm)
                t_last = self.evac(bi, lambda e, o=o, tc=tc, bi=bi:
                                   e.scalar_tensor_tensor(
                                       out=self.XA[:, o * T + tc * NT:
                                                   o * T + tc * NT + NT],
                                       in0=self.PS[bi][:, 0:NT],
                                       scalar=self.col(f"b_f2{l}", o),
                                       in1=self.XA[:, o * T + tc * NT:
                                                   o * T + tc * NT + NT],
                                       op0=ALU.add, op1=ALU.add),
                                   waits=[tokm, self.t_cols], eng="vector")
                x3map[(o, tc)] = t_last
        x3map["last"] = t_last
        return x3map

    def emit_tail(self, x16_tok):
        c = self.cfg
        T, D, NT, TC, DT = c["T"], c["D"], c["NT"], c["TC"], c["DT"]
        if c["with_tail"]:
            assert D == 1024, "TT tail assumes D=1024"
            WrO, WiO, WnO = 0, 2048, 2 * 2048
            scr = 3 * 2048
            tw = self.emit_ttw(scr, WrO, WiO, WnO)
        ring = [{"off": 7680, "busy": None}, {"off": 7680 + T, "busy": None}] \
            if c["with_tail"] else \
            [{"off": 0, "busy": None}, {"off": T, "busy": None}]
        t_mid = None
        midmap = {}
        for o in range(DT):
            r = ring[o % 2]
            xt = self.trr(r["off"], T)
            tl = self.dma("sync", xt, self.d["x_in"][o * 128:(o + 1) * 128, :],
                          waits=[r["busy"]])
            t_mid = self.op("vector", lambda e, o=o, xt=xt: e.tensor_tensor(
                out=self.XA[:, o * T:(o + 1) * T],
                in0=self.XA[:, o * T:(o + 1) * T], in1=xt, op=ALU.add),
                waits=[tl] + [self.tmap_get(x16_tok, o, tc)
                              for tc in range(TC)], inc="vp")
            for tc in range(TC):
                midmap[(o, tc)] = t_mid
            r["busy"] = t_mid
        if not c["with_tail"]:
            midmap["last"] = t_mid
            return midmap
        xh_tok = self.emit_ln(self.XA, midmap, self.XB, None, None,
                              bc_waits=[tw] if c["with_tail"] else ())
        t_last = None
        for ri in range(2):
            for ot in range(4):
                for tc in range(TC):
                    bi = tc % 2
                    mms = []
                    if ri == 0:
                        mats = [(WrO, 0), (WnO, 4)]
                    else:
                        mats = [(WiO, 0), (WrO, 4)]
                    tokm = None
                    first = True
                    for mo, dbase in mats:
                        for dc4 in range(4):
                            lhsT = self.TR[:, mo + dc4 * 512 + ot * 128:
                                           mo + dc4 * 512 + ot * 128 + 128]
                            rhs = self.XB[:, (dbase + dc4) * T + tc * NT:
                                          (dbase + dc4) * T + tc * NT + NT]
                            w = ()
                            if first:
                                w = [self.tile_waitall(xh_tok, tc, DT),
                                     tw, self.bank_free[bi]]
                            tokm = self.mm(self.PS[bi][:, 0:NT], lhsT, rhs,
                                           first, (mo, dc4) == (mats[1][0], 3),
                                           waits=w,
                                           inc="tp" if (mo, dc4) == (mats[1][0], 3)
                                           else None)
                            first = False
                    o_out = ri * 4 + ot
                    fb = "fb_r" if ri == 0 else "fb_i"
                    t_last = self.evac(bi, lambda e, o_out=o_out, tc=tc, bi=bi,
                                       fb=fb, ot=ot: e.scalar_tensor_tensor(
                                           out=self.XA[:, o_out * T + tc * NT:
                                                       o_out * T + tc * NT + NT],
                                           in0=self.PS[bi][:, 0:NT],
                                           scalar=self.col(fb, ot),
                                           in1=self.XA[:, o_out * T + tc * NT:
                                                       o_out * T + tc * NT + NT],
                                           op0=ALU.add, op1=ALU.add),
                                       waits=[tokm, self.t_cols], eng="vector")
        return t_last

    def emit_ttw(self, scr, WrO, WiO, WnO):
        R8T = self.trr(scr, 128)
        R16T = self.trr(scr + 128, 128)
        ACT = self.trr(scr + 256, 32)
        ZC = self.trr(scr + 288, 16)
        TMP = self.trr(scr + 320, 512)
        G1 = self.trr(scr + 832, 512)
        G2 = self.trr(scr + 1344, 64)
        t_r8 = self.dma("sync", R8T[0:8, :], self.d["r8r16"][0:8, :])
        t_r16 = self.dma("sync", R16T[0:16, :], self.d["r8r16"][8:24, :])
        t_con = [t_r8, t_r16]
        tw_done = []
        g_busy = []
        for mi, (g1n, g2n, WO) in enumerate((("g1r", "g2r", WrO),
                                             ("g1i", "g2i", WiO))):
            tg1 = self.dma("sync", G1[0:8, :], self.d[g1n], waits=g_busy)
            tg2 = self.dma("sync", G2[0:16, :], self.d[g2n], waits=g_busy)
            for t in range(4):
                wt = self.trr(WO + t * 512, 512)
                for cc in range(4):
                    ta = self.mm(self.PS[2][:, 0:32], R8T[0:8, :],
                                 G1[0:8, (t * 4 + cc) * 32:(t * 4 + cc) * 32 + 32],
                                 True, True,
                                 waits=[tg1, t_con, self.bank_free[2]], inc="tp")
                    ea = self.evac(2, lambda e, ta=ta: e.copy(out=ACT,
                                   in_=self.PS[2][:, 0:32]), waits=[ta])
                    tzc = self.mm(self.PS[3][:, 0:16], R16T[0:16, :],
                                  G2[0:16, cc * 16: cc * 16 + 16],
                                  True, True,
                                  waits=[tg2, t_con, self.bank_free[3]], inc="tp")
                    ez = self.evac(3, lambda e, tzc=tzc: e.copy(out=ZC,
                                   in_=self.PS[3][:, 0:16]), waits=[tzc])
                    xap = ACT.unsqueeze(2).broadcast_to([128, 32, 16])
                    yap = ZC.unsqueeze(1).broadcast_to([128, 32, 16])
                    dst = wt if cc == 0 else TMP
                    tm = self.op("vector", lambda e, d=dst, xap=xap, yap=yap:
                                 e.tensor_tensor(
                                     out=d.rearrange("p (b e) -> p b e", b=32),
                                     in0=xap, in1=yap, op=ALU.mult),
                                 waits=[ea, ez], inc="vp")
                    if cc > 0:
                        tm = self.op("vector", lambda e, wt=wt: e.tensor_tensor(
                            out=wt, in0=wt, in1=TMP, op=ALU.add),
                            waits=[tm], inc="vp")
                    self.bank_free[2] = tm
                    self.bank_free[3] = tm
                    g_busy += [ta, tzc]
                tw_done.append(tm)
        for t in range(4):
            tw_done.append(self.op(
                "scalar", lambda e, t=t: e.mul(
                    out=self.trr(WnO + t * 512, 512),
                    in_=self.trr(WiO + t * 512, 512), mul=-1.0),
                waits=[tw_done[-1]], inc="sp"))
        return tw_done[-1]


def _ref_weight_dict(cfg, inputs):
    L = cfg["L"]
    wd = {}
    for l in range(L):
        for src, dst in (("W_in", "W_in"), ("W_dt", "W_dt"), ("W_xp", "W_xp"),
                         ("W_out", "W_out"), ("W_qkv", "W_qkv"), ("W_ao", "W_ao"),
                         ("W_f1", "W_f1"), ("W_f2", "W_f2"),
                         ("b_in", "b_in"), ("b_dt", "b_dt"), ("b_xp", "b_xp"),
                         ("b_out", "b_out"), ("b_qkv", "b_qkv"), ("b_ao", "b_ao"),
                         ("b_f1", "b_f1"), ("b_f2", "b_f2"),
                         ("g1", "g1"), ("be1", "be1"), ("g2", "g2"), ("be2", "be2"),
                         ("Dp", "Dp"), ("A_log", "Alog")):
            wd[f"{dst}{l}"] = np.asarray(inputs[f"L0_attn_{src}"])[l]
    wd["G1r"] = np.asarray(inputs["L0_ffn_G1_R"])
    wd["G2r"] = np.asarray(inputs["L0_ffn_G2_R"])
    wd["G1i"] = np.asarray(inputs["L0_ffn_G1_I"])
    wd["G2i"] = np.asarray(inputs["L0_ffn_G2_I"])
    wd["fbias_r"] = np.asarray(inputs["L0_ffn_bias_real"])
    wd["fbias_i"] = np.asarray(inputs["L0_ffn_bias_imag"])
    return wd


def run_cores(cfg, wd, x_full, stage="full"):
    B, S, D = x_full.shape
    T = cfg["T"]
    assert S == 2 * T and B * 2 == 8
    wm, colidx = host_prepare_weights(cfg, wd)
    cfg = dict(cfg)
    cfg["ncol"] = wm["colsimg"].shape[1]
    cfg["colidx"] = colidx
    bld = Builder(cfg, colidx, stage=stage)
    nc = bld.build()
    in_maps = []
    for core in range(8):
        b_, hh = core // 2, core % 2
        m = dict(wm)
        xslice = np.asarray(x_full[b_, hh * T:(hh + 1) * T, :], np.float32)
        m["x_in"] = np.ascontiguousarray(xslice.T)
        m["maskcol"] = np.full((cfg["NST"], 1), float(hh), np.float32)
        m["pinfo"] = np.full((1, 1), 1 - hh, np.int32)
        in_maps.append(m)
    res = run_bass_kernel_spmd(nc, in_maps, core_ids=list(range(8)))
    out = np.empty((B, S, D), np.float32)
    for core in range(8):
        b_, hh = core // 2, core % 2
        out[b_, hh * T:(hh + 1) * T, :] = res.results[core]["y"].T
    return out


def kernel(**inputs):
    cfg = make_cfg(D=1024, T=2048, H=16, L=4)
    wd = _ref_weight_dict(cfg, inputs)
    x_full = np.asarray(inputs["L0_attn_x"], np.float32)
    return run_cores(cfg, wd, x_full, stage="full")


# version 09
# speedup vs baseline: 1.1187x; 1.1187x over previous
import contextlib
import numpy as np

import concourse.bass as bass
import concourse.mybir as mybir
from concourse.bass_utils import run_bass_kernel_spmd

F32 = mybir.dt.float32
I32 = mybir.dt.int32
AF = mybir.ActivationFunctionType
ALU = mybir.AluOpType
ENGINES = ("sync", "tensor", "vector", "scalar", "gpsimd")
RG2 = [[0, 1], [2, 3], [4, 5], [6, 7]]


def make_cfg(D=1024, T=2048, H=16, L=4, with_tail=True):
    DH, NST = 64, 64
    assert D == H * DH and H % 2 == 0
    NT = min(512, T)
    return dict(
        D=D, T=T, H=H, L=L, NST=NST, DH=DH,
        DI=2 * D, DF=4 * D, NT=NT,
        SCALE=1.0 / np.sqrt(DH), EPS=1e-5,
        DT=D // 128, DIT=(2 * D) // 128, DFT=(4 * D) // 128,
        TC=T // NT, NP=H // 2, KC=T // 128, QS=T // NT,
        with_tail=with_tail,
    )


def w_image_omajor(W):
    O, D = W.shape
    A = np.asarray(W, np.float32).reshape(O // 128, 128, D // 128, 128)
    return np.ascontiguousarray(A.transpose(0, 3, 2, 1).reshape(O, D))


def w_image_dmajor(W):
    return np.ascontiguousarray(np.asarray(W, np.float32).T)


def wxp_image(Wxp, DIT, NST):
    A = np.asarray(Wxp, np.float32).T.reshape(DIT, 128, NST)
    return np.ascontiguousarray(A.transpose(1, 0, 2).reshape(128, DIT * NST))


def build_cols_image(cfg, wd):
    cols, idx = [], {}

    def add(name, vec):
        v = np.asarray(vec, np.float32).ravel()
        n = (len(v) + 127) // 128
        idx[name] = len(cols)
        a = np.zeros((n * 128,), np.float32)
        a[: len(v)] = v
        for r in a.reshape(n, 128):
            cols.append(r)

    D = cfg["D"]
    for l in range(cfg["L"]):
        add(f"b_in{l}", wd[f"b_in{l}"])
        add(f"b_dt{l}", wd[f"b_dt{l}"])
        add(f"b_xp{l}", wd[f"b_xp{l}"])
        add(f"b_out{l}", wd[f"b_out{l}"])
        add(f"b_q{l}", wd[f"b_qkv{l}"][:D])
        add(f"b_k{l}", wd[f"b_qkv{l}"][D:2 * D])
        add(f"b_ao{l}", wd[f"b_ao{l}"])
        add(f"b_f1{l}", wd[f"b_f1{l}"])
        add(f"b_f2{l}", wd[f"b_f2{l}"])
        add(f"g1{l}", wd[f"g1{l}"])
        add(f"be1{l}", wd[f"be1{l}"])
        add(f"g2{l}", wd[f"g2{l}"])
        add(f"be2{l}", wd[f"be2{l}"])
        add(f"Dp1{l}", 1.0 + np.asarray(wd[f"Dp{l}"], np.float32))
        add(f"Alog{l}", wd[f"Alog{l}"])
    if cfg["with_tail"]:
        add("fb_r", wd["fbias_r"])
        add("fb_i", wd["fbias_i"])
    return np.ascontiguousarray(np.stack(cols, 1)), idx


def g1_image(G1):
    out = np.zeros((8, 512), np.float32)
    for t in range(4):
        for cc in range(4):
            for p in range(8):
                out[p, (t * 4 + cc) * 32:(t * 4 + cc) * 32 + 32] = G1[0, :, cc, 8 * t + p]
    return out


def g2_image(G2):
    out = np.zeros((16, 64), np.float32)
    for cc in range(4):
        out[:, cc * 16: cc * 16 + 16] = np.asarray(G2, np.float32)[cc, :, 0, :].T
    return out


def host_prepare_weights(cfg, wd):
    D, L = cfg["D"], cfg["L"]
    m = {}
    for l in range(L):
        m[f"wi{l}"] = w_image_omajor(wd[f"W_in{l}"])
        m[f"wdt{l}"] = w_image_omajor(wd[f"W_dt{l}"])
        m[f"wxp{l}"] = wxp_image(wd[f"W_xp{l}"], cfg["DIT"], cfg["NST"])
        m[f"wout{l}"] = w_image_dmajor(wd[f"W_out{l}"])
        m[f"wq{l}"] = w_image_omajor(wd[f"W_qkv{l}"][:D])
        m[f"wk{l}"] = w_image_omajor(wd[f"W_qkv{l}"][D:2 * D])
        m[f"wv{l}"] = w_image_omajor(wd[f"W_qkv{l}"][2 * D:])
        m[f"wao{l}"] = w_image_omajor(wd[f"W_ao{l}"])
        m[f"wf1{l}"] = w_image_omajor(wd[f"W_f1{l}"])
        m[f"wf2{l}"] = w_image_omajor(wd[f"W_f2{l}"])
    img, idx = build_cols_image(cfg, wd)
    m["colsimg"] = img
    m["bvrows"] = np.ascontiguousarray(
        np.stack([np.asarray(wd[f"b_qkv{l}"][2 * D:], np.float32) for l in range(L)], 0))
    if cfg["with_tail"]:
        m["g1r"] = g1_image(wd["G1r"]); m["g2r"] = g2_image(wd["G2r"])
        m["g1i"] = g1_image(wd["G1i"]); m["g2i"] = g2_image(wd["G2i"])
        r8 = np.kron(np.eye(8, dtype=np.float32), np.ones((1, 16), np.float32))
        r16 = np.kron(np.ones((1, 8), np.float32), np.eye(16, dtype=np.float32))
        m["r8r16"] = np.ascontiguousarray(np.concatenate([r8, r16], 0))
    return m, idx


class Builder:

    def __init__(self, cfg, colidx, stage="full"):
        self.cfg = cfg
        self.colidx = colidx
        self.stage = stage
        self.nc = bass.Bass("TRN2", target_bir_lowering=False)
        self.ops = {e: [] for e in ENGINES}
        self.sh, self.sc = {}, {}
        self.d = {}
        self.hold = {}
        self.prev_tok = {}
        self.dlast = {}
        self.bank_free = [None] * 8

    @staticmethod
    def _flat(waits):
        out = []
        for w in waits:
            if w is None:
                continue
            if isinstance(w, list):
                out.extend(x for x in w if x is not None)
            else:
                out.append(w)
        return tuple(out)

    ESEM = {"tensor": "tp", "vector": "vp", "scalar": "sp", "gpsimd": "gp"}

    def op(self, eng, fn, waits=(), inc=None, amt=1, is_dma=False):
        ws = self._flat(waits)
        chain = (not is_dma) and eng in ("vector", "scalar", "gpsimd")
        if chain:
            prev = self.prev_tok.get(eng)
            if prev is not None:
                ws = ws + (prev,)
            if inc is None:
                inc = self.ESEM[eng]
        tok = None
        if inc is not None:
            self.sc[inc] += amt
            tok = (inc, self.sc[inc])
        if chain:
            self.prev_tok[eng] = tok
        if getattr(self, "t_zero", None) and eng != "vector":
            ws = ws + tuple(self.t_zero)
        self.ops[eng].append((fn, ws, inc, tok, amt))
        return tok

    def realize(self, block):
        for eng in ENGINES:
            ops = self.ops[eng]
            if not ops:
                continue

            def make_body(ops):
                def body(e):
                    hi = {}
                    for fn, waits, inc, tok, amt in ops:
                        for s, v in waits:
                            if hi.get(s, -1) < v:
                                e.wait_ge(self.sh[s], v)
                                hi[s] = v
                        inst = fn(e)
                        if inc is not None:
                            inst.then_inc(self.sh[inc], amt)
                return body
            getattr(block, eng)(make_body(ops))

    def dsem(self):
        s = self.dpool[self.di % len(self.dpool)]
        self.di += 1
        return s

    def dma(self, q, out, in_, waits=(), sem=None):
        sem = sem or self.dsem()
        prev = self.dlast.get(sem)
        tok = self.op(q, lambda e: e.dma_start(out=out, in_=in_),
                      waits=self._flat(waits) + ((prev,) if prev else ()),
                      inc=sem, amt=16, is_dma=True)
        self.dlast[sem] = tok
        return tok

    def bcast(self, row_ap, dst_ap, npart, waits=()):
        sem = self.dsem()
        prev = self.dlast.get(sem)
        tok = None
        for qq in range(npart // 32):
            seed = dst_ap[qq * 32: qq * 32 + 1, :]
            tok = self.op(
                "scalar", lambda e, s=seed: e.dma_start(out=s, in_=row_ap),
                waits=(list(waits) + self.t_zero + ([prev] if prev else []))
                if qq == 0 else (),
                inc=sem, amt=16, is_dma=True)
        self.dlast[sem] = tok
        return self.op(
            "vector", lambda e: e.stream_shuffle(out=dst_ap, in_=dst_ap, mask=[0] * 32),
            waits=[tok], inc="vp")

    def load_slab(self, src_ap, ncols, waits=(), ring=None):
        if ring is None:
            s = self.wslots[self.wi % len(self.wslots)]
            self.wi += 1
        else:
            s = ring["slots"][ring["i"] % len(ring["slots"])]
            ring["i"] += 1
        tok = self.dma("sync", s["ap"][:, :ncols], src_ap,
                       waits=tuple(s["free"]) + tuple(waits))
        s["loaded"] = tok
        s["free"] = []
        return s

    def mm(self, out, lhsT, rhs, start, stop, waits=(), inc=None, tp=None, skip=False):
        kw = {}
        if tp is not None:
            kw["tile_position"] = tp
        if skip:
            kw["skip_group_check"] = True
        return self.op(
            "tensor",
            lambda e: e.matmul(out, lhsT, rhs, start=start, stop=stop, **kw),
            waits=waits, inc=inc)

    def evac(self, bank_i, fn, waits, eng="scalar"):
        tok = self.op(eng, fn, waits=waits, inc="sp" if eng == "scalar" else "vp")
        self.bank_free[bank_i] = tok
        return tok

    def col(self, name, j=0):
        i = self.colidx[name] + j
        return self.SM[:, i: i + 1]

    def trr(self, off, n):
        assert off + n <= self.TRN, (off, n)
        return self.TR[:, off: off + n]

    def row(self, strip, sl=None, np_=1):
        T = self.cfg["T"]
        if sl is None:
            return self.ROWS[0:np_, strip * T: strip * T + T]
        a, b_, st = sl.indices(T)
        return self.ROWS[0:np_, strip * T + a: strip * T + b_]

    def build(self):
        c = self.cfg
        D, T, L, NST = c["D"], c["T"], c["L"], c["NST"]
        DI, DF, NT = c["DI"], c["DF"], c["NT"]
        DT, DIT = c["DT"], c["DIT"]
        nc = self.nc

        din = lambda n, sh, dt=F32: self.d.setdefault(
            n, nc.dram_tensor(n, list(sh), dt, kind="ExternalInput").ap())
        din("x_in", [D, T])
        for l in range(L):
            din(f"wi{l}", [2 * DI, D]); din(f"wdt{l}", [D, DI])
            din(f"wxp{l}", [128, DIT * NST]); din(f"wout{l}", [DI, D])
            din(f"wq{l}", [D, D]); din(f"wk{l}", [D, D]); din(f"wv{l}", [D, D])
            din(f"wao{l}", [D, D]); din(f"wf1{l}", [DF, D]); din(f"wf2{l}", [D, DF])
        din("colsimg", [128, c["ncol"]])
        din("bvrows", [L, D])
        if c["with_tail"]:
            din("g1r", [8, 512]); din("g2r", [16, 64])
            din("g1i", [8, 512]); din("g2i", [16, 64])
            din("r8r16", [24, 128])
        din("maskcol", [NST, 1])
        din("pinfo", [1, 1], I32)
        self.y = nc.dram_tensor("y", [D, T], F32, kind="ExternalOutput").ap()

        self.kv_loc = nc.dram_tensor("kv_loc", [2 * D, T], F32)
        self.kv_all = nc.dram_tensor("kv_all", [2, 2 * D, T], F32)
        self.cr_loc = nc.dram_tensor("cr_loc", [NST, 1], F32)
        self.cr_all = nc.dram_tensor("cr_all", [2, NST, 1], F32)
        self.q_spill = nc.dram_tensor("q_spill", [D, T], F32)
        self.n_spill = nc.dram_tensor("n_spill", [2 * c["NP"], T], F32)

        with contextlib.ExitStack() as ctx:
            self.XA = ctx.enter_context(nc.sbuf_tensor("XA", [128, DT * T], F32))
            xbn = max(DT * T, (DF // 128) * NT)
            self.XB = ctx.enter_context(nc.sbuf_tensor("XB", [128, xbn], F32))
            mnt = min(256, T)
            xi_cols = (DI // 128) * mnt
            self.TRN = max(xi_cols + 2 * T + NT, 5 * T + 4 * NT,
                           (7680 + 2 * T) if c["with_tail"] else 0)
            self.TR = ctx.enter_context(nc.sbuf_tensor("TR", [128, self.TRN], F32))
            self.WB = ctx.enter_context(nc.sbuf_tensor("WB", [128, 2048], F32))
            smn = c["ncol"] + 2 * T + 2 * NT + 64
            self.SM = ctx.enter_context(nc.sbuf_tensor("SM", [128, smn], F32))
            for s in ("tp", "vp", "sp", "gp"):
                self.sh[s] = ctx.enter_context(nc.semaphore(s))
                self.sc[s] = 0
            self.dpool = []
            for i in range(84):
                nm = f"d{i}"
                self.sh[nm] = ctx.enter_context(nc.semaphore(nm))
                self.sc[nm] = 0
                self.dpool.append(nm)
            self.di = 0
            self.ccpool = []
            for i in range(12):
                nm = f"cc{i}"
                self.sh[nm] = ctx.enter_context(nc.semaphore(nm))
                self.sc[nm] = 0
                self.ccpool.append(nm)
            self.cci = 0
            self.PS = [ctx.enter_context(nc.psum_tensor(f"pb{i}", [128, 512], F32))
                       for i in range(8)]
            self.preg = ctx.enter_context(nc.sync.register("preg"))
            self.wslots = [{"ap": self.WB[:, i * 1024:(i + 1) * 1024],
                            "loaded": None, "free": []} for i in range(2)]
            self.wi = 0
            ncol = c["ncol"]
            self.ROWS = self.SM[:, ncol: ncol + 2 * T]
            self.NACC = self.SM[:, ncol + 2 * T: ncol + 2 * T + 2 * NT]
            self.MISC = self.SM[:, ncol + 2 * T + 2 * NT:
                                ncol + 2 * T + 2 * NT + 64]

            self.emit_all()

            block = ctx.enter_context(nc.Block())
            self.realize(block)
        return nc

    def emit_all(self):
        c = self.cfg
        T, D, L, DT, NST = c["T"], c["D"], c["L"], c["DT"], c["NST"]

        tz1 = self.op("vector", lambda e: e.memset(self.TR[:], 0.0), inc="vp")
        tz2 = self.op("vector", lambda e: e.memset(
            self.SM[:, c["ncol"]:], 0.0), inc="vp")
        self.t_zero = [tz1, tz2]
        self.t_cols = self.dma("sync", self.SM[:, 0:c["ncol"]], self.d["colsimg"])
        self.onescol = self.MISC[:, 0:1]
        self.t_ones = self.op("vector", lambda e: e.memset(self.onescol, 1.0), inc="vp")
        tz = self.op("vector", lambda e: e.memset(self.MISC[:, 1:3], 0.0), inc="vp")
        te1 = self.op("vector", lambda e: e.memset(self.MISC[0:64, 1:2], 1.0),
                      waits=[tz], inc="vp")
        self.t_e2 = self.op("vector", lambda e: e.memset(self.MISC[64:128, 2:3], 1.0),
                            waits=[te1], inc="vp")
        self.E2 = self.MISC[:, 1:3]

        t49 = self.op("vector", lambda e: e.memset(self.MISC[:, 49:50], 1.0), inc="vp")
        t50 = self.op("vector", lambda e: e.memset(self.MISC[:, 50:51], 0.0), inc="vp")
        t51 = self.op("vector", lambda e: e.memset(self.MISC[:, 51:52], 1.0), inc="vp")
        self.E01 = self.MISC[:, 49:51]
        self.E10 = self.MISC[:, 50:52]
        self.t_e01 = t51
        self.negones = self.MISC[:, 60:61]
        self.t_negones = self.op("vector", lambda e: e.memset(self.negones, -1.0),
                                 inc="vp")
        self.epscol = self.MISC[:, 48:49]
        self.t_eps = self.op("vector", lambda e: e.memset(self.epscol, self.cfg["EPS"]),
                             inc="vp")

        def loadreg(e):
            e.load(self.preg, self.d["pinfo"][0:1, 0:1])
            self.hold["pidx"] = e.snap(self.preg, min_val=0, max_val=1)
            return e.nop()
        self.op("sync", loadreg)

        xin_toks = []
        for o in range(DT):
            xin_toks.append(self.dma("sync", self.XA[:, o * T:(o + 1) * T],
                                     self.d["x_in"][o * 128:(o + 1) * 128, :]))
        self.t_mask = self.dma("sync", self.MISC[0:NST, 3:4], self.d["maskcol"])

        self.acol = {}
        for l in range(L):
            dst = self.MISC[:, 8 + l: 9 + l]
            t1 = self.op("scalar", lambda e, s=self.col(f"Alog{l}"), d=dst:
                         e.activation(out=d, in_=s, func=AF.Exp),
                         waits=[self.t_cols], inc="sp")
            t2 = self.op("scalar", lambda e, d=dst:
                         e.mul(out=d, in_=d, mul=-1.0 / D), waits=[t1], inc="sp")
            self.acol[l] = (dst, t2)

        cur = self.emit_ln(self.XA, xin_toks, self.XA, None, None)
        if self.stage == "ln":
            return self.emit_output(cur)

        if self.stage == "ffnonly":
            cur = self.emit_ffn(0, cur)
            return self.emit_output(cur)
        nl = L if self.stage in ("full",) else 1
        for l in range(nl):
            cur = self.emit_mamba(l, cur)
            if self.stage == "mamba" and l == 0:
                return self.emit_output(cur)
            cur = self.emit_attention(l, cur)
            if self.stage == "attn" and l == 0:
                return self.emit_output(cur)
            cur = self.emit_ffn(l, cur)
            if self.stage == "ffn" and l == 0:
                return self.emit_output(cur)
        cur = self.emit_tail(cur)
        return self.emit_output(cur)

    def emit_output(self, tok):
        c = self.cfg
        T, DT = c["T"], c["DT"]
        if isinstance(tok, dict):
            tok = tok["last"]
        toks = []
        for o in range(DT):
            toks.append(self.dma("scalar", self.y[o * 128:(o + 1) * 128, :],
                                 self.XA[:, o * T:(o + 1) * T], waits=[tok]))
        self.op("sync", lambda e: e.nop(), waits=toks)
        return toks[-1]

    def tile_waitall(self, tok_or_map, tc, n_o):
        if not isinstance(tok_or_map, dict):
            return tok_or_map
        best = None
        for o in range(n_o):
            t = tok_or_map.get((o, tc))
            if t is not None and (best is None or
                                  (t[0] == best[0] and t[1] > best[1])):
                best = t
        return best

    @staticmethod
    def tmap_get(tok_or_map, o, tc):
        if isinstance(tok_or_map, dict):
            return tok_or_map.get((o, tc))
        return tok_or_map

    def emit_ln(self, src, src_tok, dst, gname, bname, bc_waits=()):
        c = self.cfg
        T, D, DT, TC, NT, EPS = c["T"], c["D"], c["DT"], c["TC"], c["NT"], c["EPS"]
        ev = []
        sq_busy = [None, None]
        for tc in range(TC):
            tokA = None
            for o in range(DT):
                stile = src[:, o * T + tc * NT: o * T + tc * NT + NT]
                w = [self.tmap_get(src_tok, o, tc)]
                if o == 0 and tc == 0:
                    w += list(bc_waits)
                if o == 0:
                    w += [self.bank_free[6], self.t_ones]
                tokA = self.mm(self.PS[6][0:1, 0:NT], self.onescol, stile,
                               o == 0, o == DT - 1, waits=w,
                               inc="tp" if o == DT - 1 else None)
            tokB = None
            for o in range(DT):
                stile = src[:, o * T + tc * NT: o * T + tc * NT + NT]
                sq = self.NACC[:, (o % 2) * NT: (o % 2) * NT + NT]
                w = [self.tmap_get(src_tok, o, tc), sq_busy[o % 2]]
                if o == 0 and tc == 0:
                    w += list(bc_waits)
                tq = self.op("scalar", lambda e, s=stile, q=sq: e.activation(
                    out=q, in_=s, func=AF.Square), waits=w, inc="sp")
                w2 = [tq] + ([self.bank_free[7]] if o == 0 else [])
                tokB = self.mm(self.PS[7][0:1, 0:NT], self.onescol, sq,
                               o == 0, o == DT - 1, waits=w2, inc="tp")
                sq_busy[o % 2] = tokB
            ev.append(self.evac(6, lambda e, tc=tc: e.copy(
                out=self.row(0, slice(tc * NT, tc * NT + NT)),
                in_=self.PS[6][0:1, 0:NT]), waits=[tokA]))
            ev.append(self.evac(7, lambda e, tc=tc: e.copy(
                out=self.row(1, slice(tc * NT, tc * NT + NT)),
                in_=self.PS[7][0:1, 0:NT]), waits=[tokB]))
        s0 = self.row(0)
        s1 = self.row(1)
        t1 = self.op("scalar", lambda e: e.mul(out=s0, in_=s0, mul=1.0 / D),
                     waits=ev, inc="sp")
        t2 = self.op("scalar", lambda e: e.mul(out=s1, in_=s1, mul=1.0 / D),
                     waits=ev, inc="sp")
        tv = [t1, t2] + sq_busy
        for tc in range(TC):
            tsl = slice(tc * NT, tc * NT + NT)
            s2 = self.NACC[0:1, 0:NT]
            ta_ = self.op("vector", lambda e, s2=s2, tsl=tsl: e.tensor_tensor(
                out=s2, in0=s0[:, tsl], in1=s0[:, tsl], op=ALU.mult),
                waits=tv, inc="vp")
            tv = [self.op("vector", lambda e, s2=s2, tsl=tsl: e.tensor_tensor(
                out=s1[:, tsl], in0=s1[:, tsl], in1=s2, op=ALU.subtract),
                waits=[ta_], inc="vp")]
        t5 = self.op("scalar", lambda e: e.activation(
            out=s1, in_=s1, func=AF.Ln, bias=self.epscol[0:1, :]),
            waits=tv, inc="sp")
        t6 = self.op("scalar", lambda e: e.activation(
            out=s1, in_=s1, func=AF.Exp, scale=-0.5), waits=[t5], inc="sp")
        t7 = self.op("vector", lambda e: e.tensor_tensor(
            out=s0, in0=s0, in1=s1, op=ALU.mult), waits=[t6], inc="vp")
        t8 = self.op("vector", lambda e: e.tensor_scalar(
            out=s0, in0=s0, scalar1=-1.0, scalar2=None, op0=ALU.mult),
            waits=[t7], inc="vp")
        out_tok = None
        tmap = {}
        slot_busy = []
        for tc in range(TC):
            c1b = self.NACC[:, 0:NT]
            c0b = self.NACC[:, NT:2 * NT]
            tsl = slice(tc * NT, tc * NT + NT)
            tb1 = self.bcast(s1[:, tsl], c1b, 128, waits=[t8] + slot_busy)
            tb0 = self.bcast(s0[:, tsl], c0b, 128, waits=[tb1])
            last_ta = None
            for o in range(DT):
                sl = slice(o * T + tc * NT, o * T + tc * NT + NT)
                stile, dtile = src[:, sl], dst[:, sl]
                tm = self.op("vector", lambda e, s=stile, d=dtile, c1b=c1b:
                             e.tensor_tensor(out=d, in0=s, in1=c1b, op=ALU.mult),
                             waits=[tb1, tb0], inc="vp")
                ta = self.op("vector", lambda e, d=dtile, c0b=c0b:
                             e.tensor_tensor(out=d, in0=d, in1=c0b, op=ALU.add),
                             waits=[tm], inc="vp")
                if gname is not None:
                    out_tok = self.op("scalar", lambda e, d=dtile, o=o: e.activation(
                        out=d, in_=d, func=AF.Identity, bias=self.col(bname, o),
                        scale=self.col(gname, o)), waits=[ta, self.t_cols], inc="sp")
                else:
                    out_tok = ta
                last_ta = ta
                tmap[(o, tc)] = out_tok
            slot_busy = [last_ta]
        tmap["last"] = out_tok
        return tmap

    def emit_mamba(self, l, x0_tok):
        c = self.cfg
        T, D, NT, TC, NST = c["T"], c["D"], c["NT"], c["TC"], c["NST"]
        DT, DIT = c["DT"], c["DIT"]
        MNT = min(256, T)
        MC = T // MNT
        xi_cols = DIT * MNT
        AU = self.trr(xi_cols, 2 * T)
        zoff = xi_cols + 2 * T
        ring = {"slots": list(self.wslots), "i": 0}
        if self.TRN - (zoff + 2 * MNT) >= 2048:
            seed = [self._engine_now("vector")]
            ring["slots"] += [
                {"ap": self.trr(zoff + 2 * MNT + i * 1024, 1024),
                 "loaded": None, "free": list(seed)} for i in range(2)]
        deep = len(ring["slots"]) == 4
        z_busy = [None, None]
        xi_busy = None
        m_ev, u_ev, mo_ev = [], [], []
        last_wout = None
        nbd_tok = []
        for o in range(DT):
            nbd_tok.append(self.op(
                "scalar", lambda e, o=o: e.mul(out=self.MISC[:, 52 + o: 53 + o],
                                               in_=self.col(f"b_dt{l}", o), mul=-1.0),
                waits=[self.t_cols], inc="sp"))

        for mc in range(MC):
            xi = [self.trr(dc * MNT, MNT) for dc in range(DIT)]
            xi_toks = [None] * DIT
            xa_sl = lambda dc: self.XA[:, dc * T + mc * MNT: dc * T + mc * MNT + MNT]
            for os_ in range(DIT):
                slab = self.load_slab(self.d[f"wi{l}"][os_ * 128:(os_ + 1) * 128, :],
                                      D, ring=ring)
                bi = os_ % 2
                tokm = None
                for dc in range(DT):
                    w = ()
                    if dc == 0:
                        w = [self.tile_waitall(x0_tok, (mc * MNT) // NT, DT),
                             slab["loaded"], self.bank_free[bi], xi_busy]
                    tokm = self.mm(self.PS[bi][:, 0:MNT],
                                   slab["ap"][:, dc * 128:(dc + 1) * 128],
                                   xa_sl(dc), dc == 0, dc == DT - 1, waits=w,
                                   inc="tp" if dc == DT - 1 else None)
                slab["free"].append(tokm)
                xi_toks[os_] = self.evac(
                    bi, lambda e, q=xi[os_], bi=bi, os_=os_: e.activation(
                        out=q, in_=self.PS[bi][:, 0:MNT], func=AF.Silu,
                        bias=self.col(f"b_in{l}", os_)),
                    waits=[tokm, self.t_cols])
            tok4 = None
            pendm = None

            def emit_mfold(po, ptsp2, psp_tile):
                w4 = [ptsp2, self.t_negones] + \
                    ([self.bank_free[4]] if po == 0 else [])
                t4 = self.mm(self.PS[4][0:1, 0:MNT], self.negones, psp_tile,
                             po == 0, po == DT - 1, waits=w4,
                             inc="tp" if po == DT - 1 else None)
                z_busy[po % 2] = t4
                return t4

            for o in range(DT):
                srcw = self.d[f"wdt{l}"][o * 128:(o + 1) * 128, :]
                npc = (DIT * 128 + 1023) // 1024
                pieces = []
                for pc in range(npc):
                    cc = min(1024, DIT * 128 - pc * 1024)
                    pieces.append(self.load_slab(srcw[:, pc * 1024: pc * 1024 + cc],
                                                 cc, ring=ring))
                bi = 2 + o % 2
                tokm = None
                for dc in range(DIT):
                    pc, off = divmod(dc * 128, 1024)
                    w = []
                    if off == 0:
                        w.append(pieces[pc]["loaded"])
                    if dc == 0:
                        w += [self.bank_free[bi]]
                    if o == 0:
                        w.append(xi_toks[dc])
                    tokm = self.mm(self.PS[bi][:, 0:MNT],
                                   pieces[pc]["ap"][:, off: off + 128],
                                   xi[dc], dc == 0, dc == DIT - 1, waits=w,
                                   inc="tp" if dc == DIT - 1 else None)
                for pcs in pieces:
                    pcs["free"].append(tokm)
                sp_tile = self.trr(zoff + (o % 2) * MNT, MNT)
                tsp = self.op("scalar", lambda e, q=sp_tile, bi=bi, o=o: e.activation(
                    out=q, in_=self.PS[bi][:, 0:MNT], func=AF.Sigmoid, scale=-1.0,
                    bias=self.MISC[:, 52 + o: 53 + o]),
                    waits=[tokm, nbd_tok[o], z_busy[o % 2]], inc="sp")
                self.bank_free[bi] = tsp
                tsp2 = self.op("scalar", lambda e, q=sp_tile: e.activation(
                    out=q, in_=q, func=AF.Ln), waits=[tsp], inc="sp")
                if pendm is not None:
                    emit_mfold(*pendm)
                pendm = (o, tsp2, sp_tile)
            tok4 = emit_mfold(*pendm)
            m_ev.append(self.evac(4, lambda e, mc=mc: e.copy(
                out=self.row(0, slice(mc * MNT, mc * MNT + MNT)),
                in_=self.PS[4][0:1, 0:MNT]), waits=[tok4]))
            slab = self.load_slab(self.d[f"wxp{l}"], DIT * NST, ring=ring)
            tokm = None
            for dc in range(DIT):
                w = ([slab["loaded"], xi_toks[-1], self.bank_free[5]]
                     if dc == 0 else ())
                tokm = self.mm(self.PS[5][0:NST, 0:MNT],
                               slab["ap"][0:128, dc * NST:(dc + 1) * NST],
                               xi[dc], dc == 0, dc == DIT - 1, waits=w,
                               inc="tp" if dc == DIT - 1 else None)
            slab["free"].append(tokm)
            u_ev.append(self.evac(5, lambda e, mc=mc: e.activation(
                out=AU[0:NST, T + mc * MNT: T + mc * MNT + MNT],
                in_=self.PS[5][0:NST, 0:MNT], func=AF.Identity,
                bias=self.col(f"b_xp{l}")[0:NST, :]),
                waits=[tokm, self.t_cols]))
            pendo = None

            def emit_wout(pos, ptz, pz, pslab2):
                nonlocal last_wout
                for o in range(DT):
                    bi2 = 4 + o // 2
                    colh = (o % 2) * MNT
                    st = (pos == 0 and o % 2 == 0)
                    w = [ptz, pslab2["loaded"]]
                    if st:
                        w.append(self.bank_free[bi2])
                    stopf = (pos == DIT - 1 and o % 2 == 1)
                    last_wout = self.mm(
                        self.PS[bi2][:, colh: colh + MNT],
                        pslab2["ap"][:, o * 128:(o + 1) * 128], pz,
                        st, stopf, waits=w, inc="tp")
                pslab2["free"].append(last_wout)
                z_busy[pos % 2] = last_wout

            for os_ in range(DIT):
                slab = self.load_slab(
                    self.d[f"wi{l}"][(DIT + os_) * 128:(DIT + os_ + 1) * 128, :],
                    D, ring=ring)
                bi = os_ % 2
                tokm = None
                for dc in range(DT):
                    w = ()
                    if dc == 0:
                        w = [self.tile_waitall(x0_tok, (mc * MNT) // NT, DT),
                             slab["loaded"], self.bank_free[bi]]
                    tokm = self.mm(self.PS[bi][:, 0:MNT],
                                   slab["ap"][:, dc * 128:(dc + 1) * 128],
                                   xa_sl(dc), dc == 0, dc == DT - 1, waits=w,
                                   inc="tp" if dc == DT - 1 else None)
                slab["free"].append(tokm)
                z = self.trr(zoff + (os_ % 2) * MNT, MNT)
                tsil = self.op("scalar", lambda e, z=z, bi=bi, os_=os_: e.activation(
                    out=z, in_=self.PS[bi][:, 0:MNT], func=AF.Silu,
                    bias=self.col(f"b_in{l}", DIT + os_)),
                    waits=[tokm, self.t_cols, z_busy[os_ % 2]], inc="sp")
                self.bank_free[bi] = tsil
                tz = self.op("vector", lambda e, z=z, q=xi[os_]: e.tensor_tensor(
                    out=z, in0=z, in1=q, op=ALU.mult),
                    waits=[tsil, xi_toks[os_]], inc="vp")
                if deep:
                    slab2 = self.load_slab(
                        self.d[f"wout{l}"][os_ * 128:(os_ + 1) * 128, :],
                        D, ring=ring)
                    if pendo is not None:
                        emit_wout(*pendo)
                else:
                    if pendo is not None:
                        emit_wout(*pendo)
                    slab2 = self.load_slab(
                        self.d[f"wout{l}"][os_ * 128:(os_ + 1) * 128, :],
                        D, ring=ring)
                pendo = (os_, tz, z, slab2)
            emit_wout(*pendo)
            for o in range(DT):
                bi2 = 4 + o // 2
                colh = (o % 2) * MNT
                mo_ev.append(self.evac(bi2, lambda e, o=o, bi2=bi2, colh=colh, mc=mc:
                                       e.copy(out=self.XB[:, o * T + mc * MNT:
                                                          o * T + mc * MNT + MNT],
                                              in_=self.PS[bi2][:, colh: colh + MNT]),
                                       waits=[last_wout]))
            xi_busy = last_wout

        SCOUT = AU[64:128, 0:T]
        SCOUT2 = AU[64:128, T:2 * T]
        tb = self.bcast(self.row(0), AU[0:64, 0:T], 64, waits=[m_ev[-1]])
        ta = self.op("scalar", lambda e: e.activation(
            out=AU[0:64, 0:T], in_=AU[0:64, 0:T], func=AF.Exp,
            scale=self.acol[l][0][0:64, :]),
            waits=[tb, self.acol[l][1]], inc="sp")
        t_s1 = self.op("vector", lambda e: e.tensor_tensor_scan(
            out=SCOUT[0:NST, :], data0=AU[0:NST, 0:T], data1=AU[0:NST, T:2 * T],
            initial=0.0, op0=ALU.mult, op1=ALU.add),
            waits=[ta, u_ev], inc="vp")
        t_cr = self.dma("scalar", self.cr_loc.ap(), SCOUT[0:NST, T - 1: T],
                        waits=[t_s1])
        ccs = self.ccpool[self.cci % len(self.ccpool)]; self.cci += 1
        t_ag = self.op("gpsimd", lambda e: e.collective_compute(
            "AllGather", ALU.bypass, replica_groups=RG2,
            ins=[self.cr_loc.ap().opt()], outs=[self.cr_all.ap().opt()]),
            waits=[t_cr], inc=ccs)
        t_cl = self.dma("scalar", self.MISC[0:NST, 4:5], self.cr_all.ap()[0],
                        waits=[t_ag])
        t_init = self.op("vector", lambda e: e.tensor_tensor(
            out=self.MISC[0:NST, 5:6], in0=self.MISC[0:NST, 4:5],
            in1=self.MISC[0:NST, 3:4], op=ALU.mult),
            waits=[t_cl, self.t_mask], inc="vp")
        t_s2 = self.op("vector", lambda e: e.tensor_tensor_scan(
            out=SCOUT2[0:NST, :], data0=AU[0:NST, 0:T], data1=AU[0:NST, T:2 * T],
            initial=self.MISC[0:NST, 5:6], op0=ALU.mult, op1=ALU.add),
            waits=[t_init], inc="vp")
        ss_ev = []
        for tc in range(TC):
            w = [t_s2, self.bank_free[0], self.t_ones]
            tokm = self.mm(self.PS[0][0:1, 0:NT], self.onescol[64:128, :],
                           SCOUT2[0:NST, tc * NT: tc * NT + NT],
                           True, True, waits=w, inc="tp", tp=(64, 0))
            ss_ev.append(self.evac(0, lambda e, tc=tc: e.copy(
                out=self.row(1, slice(tc * NT, tc * NT + NT)),
                in_=self.PS[0][0:1, 0:NT]), waits=[tokm]))

        t_last = None
        smap = {}
        sb_busy = []
        for tc in range(TC):
            sb = self.trr(zoff, NT)
            tb = self.bcast(self.row(1, slice(tc * NT, tc * NT + NT)),
                            sb, 128,
                            waits=[ss_ev[tc], z_busy[0], z_busy[1]] + sb_busy)
            for o in range(DT):
                sl = slice(o * T + tc * NT, o * T + tc * NT + NT)
                mo = self.XB[:, sl]
                xa = self.XA[:, sl]
                t1 = self.op("vector", lambda e, mo=mo, sb=sb: e.tensor_tensor(
                    out=mo, in0=mo, in1=sb, op=ALU.mult),
                    waits=[tb, mo_ev], inc="vp")
                t2 = self.op("vector", lambda e, xa=xa, o=o: e.tensor_scalar(
                    out=xa, in0=xa, scalar1=self.col(f"Dp1{l}", o),
                    scalar2=self.col(f"b_out{l}", o), op0=ALU.mult, op1=ALU.add),
                    waits=[self.t_cols, t1], inc="vp")
                t_last = self.op("vector", lambda e, xa=xa, mo=mo: e.tensor_tensor(
                    out=xa, in0=xa, in1=mo, op=ALU.add),
                    waits=[t2], inc="vp")
                smap[(o, tc)] = t_last
                sb_busy = [t1]
        return self.emit_ln(self.XA, smap, self.XA, f"g1{l}", f"be1{l}")

    def emit_attention(self, l, x1_tok):
        c = self.cfg
        T, D, NT, TC, NP, KC, QS = (c["T"], c["D"], c["NT"], c["TC"], c["NP"],
                                    c["KC"], c["QS"])
        DT, NST, SCALE = c["DT"], c["NST"], c["SCALE"]
        QOFF = 4 * T
        PTOFF = 5 * T

        kv_dma = []
        for f in range(NP):
            slab = self.load_slab(self.d[f"wk{l}"][f * 128:(f + 1) * 128, :], D)
            for tc in range(TC):
                bi = tc % 2
                tokm = None
                for dc in range(DT):
                    w = ()
                    if dc == 0:
                        w = [self.tile_waitall(x1_tok, tc, DT),
                             slab["loaded"], self.bank_free[bi]]
                    tokm = self.mm(self.PS[bi][:, 0:NT],
                                   slab["ap"][:, dc * 128:(dc + 1) * 128],
                                   self.XA[:, dc * T + tc * NT: dc * T + tc * NT + NT],
                                   dc == 0, dc == DT - 1, waits=w,
                                   inc="tp" if dc == DT - 1 else None)
                ek = self.evac(bi, lambda e, f=f, tc=tc, bi=bi: e.activation(
                    out=self.XB[:, f * T + tc * NT: f * T + tc * NT + NT],
                    in_=self.PS[bi][:, 0:NT], func=AF.Identity, bias=self.col(f"b_k{l}", f)),
                    waits=[tokm, self.t_cols])
            slab["free"].append(tokm)
            kv_dma.append(self.dma("scalar", self.kv_loc.ap()[f * 128:(f + 1) * 128, :],
                                   self.XB[:, f * T:(f + 1) * T], waits=[ek]))
        bvb = self.trr(QOFF, D)
        t_bvb = self.bcast(self.d["bvrows"][l: l + 1, :], bvb, 128, waits=())
        v_ev = []
        kdma_all = list(kv_dma)
        for pp in range(NP):
            slab = self.load_slab(self.d[f"wv{l}"][pp * 128:(pp + 1) * 128, :], D)
            G4 = min(4, KC)
            for ttg in range(KC // G4):
                bi = ttg % 2
                vfirst = True
                tokm = None
                for dc in range(DT):
                    for j in range(G4):
                        tt = ttg * G4 + j
                        w = ()
                        if vfirst:
                            w = [slab["loaded"], self.bank_free[bi],
                                 self.tile_waitall(
                                     x1_tok, (ttg * G4 * 128) // NT, DT),
                                 self.tile_waitall(
                                     x1_tok, ((ttg + 1) * G4 * 128 - 1) // NT, DT)]
                        stopf = (dc == DT - 1 and j == G4 - 1)
                        tokm = self.mm(
                            self.PS[bi][:, j * 128:(j + 1) * 128],
                            self.XA[:, dc * T + tt * 128: dc * T + tt * 128 + 128],
                            slab["ap"][:, dc * 128:(dc + 1) * 128],
                            vfirst, stopf, waits=w, inc="tp")
                        vfirst = False
                ev_last = None
                for j in range(G4):
                    tt = ttg * G4 + j
                    ev_last = self.op(
                        "vector", lambda e, j=j, tt=tt, bi=bi, pp=pp: e.tensor_tensor(
                            out=self.XB[:, pp * T + tt * 128: pp * T + tt * 128 + 128],
                            in0=self.PS[bi][:, j * 128:(j + 1) * 128],
                            in1=bvb[:, pp * 128:(pp + 1) * 128],
                            op=ALU.add),
                        waits=[tokm, t_bvb, kdma_all], inc="vp")
                self.bank_free[bi] = ev_last
                v_ev.append(ev_last)
            slab["free"].append(tokm)
        for p in range(NP):
            kv_dma.append(self.dma(
                "scalar", self.kv_loc.ap()[D + p * 128: D + (p + 1) * 128, :],
                self.XB[:, p * T:(p + 1) * T], waits=[v_ev[-1]]))
        ccs = self.ccpool[self.cci % len(self.ccpool)]; self.cci += 1
        t_agkv = self.op("gpsimd", lambda e: e.collective_compute(
            "AllGather", ALU.bypass, replica_groups=RG2,
            ins=[self.kv_loc.ap().opt()], outs=[self.kv_all.ap().opt()]),
            waits=kv_dma, inc=ccs)

        kslot = [{"off": 0, "busy": None}, {"off": 2 * T, "busy": None}]
        vslot = [{"off": T, "busy": None}, {"off": 3 * T, "busy": None}]
        ptbusy = [None] * 4
        nacc_busy = []
        rb_busy = []
        qbusy = None
        xb_part_tok = {}
        nrow_tok = {}
        ofinal_map = {}
        q_spill_tok = [None] * NP
        si = 0
        ofinal = []
        for half in range(2):
            for hp in range(NP):
                if half == 0:
                    slab = self.load_slab(
                        self.d[f"wq{l}"][hp * 128:(hp + 1) * 128, :], D)
                    Q = self.trr(QOFF, T)
                    eq = None
                    for tc in range(TC):
                        bi = tc % 2
                        tokm = None
                        for dc in range(DT):
                            w = ()
                            if dc == 0:
                                w = [self.tile_waitall(x1_tok, tc, DT),
                                     slab["loaded"], self.bank_free[bi], qbusy,
                                     t_bvb]
                            tokm = self.mm(
                                self.PS[bi][:, 0:NT],
                                slab["ap"][:, dc * 128:(dc + 1) * 128],
                                self.XA[:, dc * T + tc * NT: dc * T + tc * NT + NT],
                                dc == 0, dc == DT - 1, waits=w,
                                inc="tp" if dc == DT - 1 else None)
                        eq = self.evac(bi, lambda e, tc=tc, bi=bi, hp=hp: e.activation(
                            out=self.trr(QOFF + tc * NT, NT), in_=self.PS[bi][:, 0:NT],
                            func=AF.Identity, bias=self.col(f"b_q{l}", hp)),
                            waits=[tokm, self.t_cols,
                                   q_spill_tok[hp - 1] if hp > 0 else None])
                    slab["free"].append(tokm)
                    q_spill_tok[hp] = self.dma(
                        "scalar", self.q_spill.ap()[hp * 128:(hp + 1) * 128, :],
                        Q, waits=[eq])
                    q_tok = eq
                else:
                    Q = self.trr(QOFF, T)
                    q_tok = self.dma(
                        "sync", Q, self.q_spill.ap()[hp * 128:(hp + 1) * 128, :],
                        waits=[q_spill_tok, qbusy])
                ks = kslot[si % 2]
                vs = vslot[si % 2]
                si += 1
                Kh = self.trr(ks["off"], T)
                Vh = self.trr(vs["off"], T)
                if half == 0:
                    ksrc = self.kv_loc.ap()[hp * 128:(hp + 1) * 128, :]
                    vsrc = self.kv_loc.ap()[D + hp * 128: D + (hp + 1) * 128, :]
                    kw = [kv_dma[hp], ks["busy"]]
                    vw = [kv_dma[NP + hp], vs["busy"]]
                else:
                    ksrc = None
                    vsrc = None
                    kw = [t_agkv, ks["busy"]]
                    vw = [t_agkv, vs["busy"]]
                if half == 0:
                    tk = self.dma("sync", Kh, ksrc, waits=kw)
                    tv = self.dma("sync", Vh, vsrc, waits=vw)
                else:
                    ksem, vsem = self.dsem(), self.dsem()
                    kprev, vprev = self.dlast.get(ksem), self.dlast.get(vsem)
                    tk = self.op("sync", lambda e, Kh=Kh, hp=hp: e.dma_start(
                        out=Kh, in_=self.kv_all.ap()[self.hold["pidx"]]
                        [hp * 128:(hp + 1) * 128, :]),
                        waits=self._flat(kw) + ((kprev,) if kprev else ()),
                        inc=ksem, amt=16, is_dma=True)
                    tv = self.op("sync", lambda e, Vh=Vh, hp=hp: e.dma_start(
                        out=Vh, in_=self.kv_all.ap()[self.hold["pidx"]]
                        [D + hp * 128: D + (hp + 1) * 128, :]),
                        waits=self._flat(vw) + ((vprev,) if vprev else ()),
                        inc=vsem, amt=16, is_dma=True)
                    self.dlast[ksem] = tk
                    self.dlast[vsem] = tv
                if half == 0:
                    tqb = self.emit_bounds(l, hp, Q, q_tok, Kh, tk, PTOFF)
                else:
                    tqb = None
                for qs in range(QS):
                    ob = 2 + qs % 2
                    pend = None

                    def emit_ctx(pk, pte1, pte2, ppt1, ppt2, ptn1, ptn2,
                                 ob=ob, tv=tv, Vh=Vh):
                        wv1 = [pte1, tv]
                        if pk == 0:
                            wv1.append(self.bank_free[ob])
                        c1 = self.mm(self.PS[ob][0:64, 0:NT],
                                     Vh[:, pk * 128: pk * 128 + 64], ppt1,
                                     pk == 0, pk == KC - 1, waits=wv1,
                                     inc="tp", tp=(0, 0), skip=True)
                        c2 = self.mm(self.PS[ob][64:128, 0:NT],
                                     Vh[:, pk * 128 + 64: pk * 128 + 128],
                                     ppt2, pk == 0, pk == KC - 1,
                                     waits=[pte2, tv] + ([self.bank_free[ob]]
                                                         if pk == 0 else []),
                                     inc="tp", tp=(0, 64), skip=True)
                        ptbusy[pk % 2] = [ptn1, c1]
                        ptbusy[2 + pk % 2] = [ptn2, c2]
                        return c1, c2

                    for kc in range(KC):
                        sb1 = (0, 1, 5)[kc % 3]
                        sb2 = (6, 7, 4)[kc % 3]
                        w1 = [tk, q_tok, self.bank_free[sb1]]
                        tm1 = self.mm(self.PS[sb1][:, 0:NT],
                                      Kh[0:64, kc * 128:(kc + 1) * 128],
                                      Q[0:64, qs * NT: qs * NT + NT],
                                      True, True, waits=w1, inc="tp", tp=(0, 0))
                        tm2 = self.mm(self.PS[sb2][:, 0:NT],
                                      Kh[64:128, kc * 128:(kc + 1) * 128],
                                      Q[64:128, qs * NT: qs * NT + NT],
                                      True, True,
                                      waits=[tk, q_tok, self.bank_free[sb2]],
                                      inc="tp", tp=(64, 0))
                        pt1 = self.trr(PTOFF + (kc % 2) * NT, NT)
                        pt2 = self.trr(PTOFF + (2 + kc % 2) * NT, NT)
                        te1 = self.op("scalar", lambda e, pt=pt1, b=sb1, hp=hp:
                                      e.activation(out=pt, in_=self.PS[b][:, 0:NT],
                                                   func=AF.Exp, scale=SCALE,
                                                   bias=self.MISC[:, 16 + 2 * hp:
                                                                  17 + 2 * hp]),
                                      waits=[tm1, tqb] + (ptbusy[kc % 2] or []),
                                      inc="sp")
                        self.bank_free[sb1] = te1
                        te2 = self.op("scalar", lambda e, pt=pt2, b=sb2, hp=hp:
                                      e.activation(out=pt, in_=self.PS[b][:, 0:NT],
                                                   func=AF.Exp, scale=SCALE,
                                                   bias=self.MISC[:, 17 + 2 * hp:
                                                                  18 + 2 * hp]),
                                      waits=[tm2, tqb] + (ptbusy[2 + kc % 2] or []),
                                      inc="sp")
                        self.bank_free[sb2] = te2
                        if kc == 0:
                            tn1 = self.op("vector", lambda e, pt=pt1: e.tensor_copy(
                                self.NACC[:, 0:NT], pt),
                                waits=[te1] + nacc_busy, inc="vp")
                            tn2 = self.op("vector", lambda e, pt=pt2: e.tensor_copy(
                                self.NACC[:, NT:2 * NT], pt), waits=[te2], inc="vp")
                            nacc_busy = []
                        else:
                            tn1 = self.op("vector", lambda e, pt=pt1: e.tensor_tensor(
                                out=self.NACC[:, 0:NT], in0=self.NACC[:, 0:NT],
                                in1=pt, op=ALU.add), waits=[te1], inc="vp")
                            tn2 = self.op("vector", lambda e, pt=pt2: e.tensor_tensor(
                                out=self.NACC[:, NT:2 * NT], in0=self.NACC[:, NT:2 * NT],
                                in1=pt, op=ALU.add), waits=[te2], inc="vp")
                        if pend is not None:
                            tc1, tc2 = emit_ctx(*pend)
                        pend = (kc, te1, te2, pt1, pt2, tn1, tn2)
                    tc1, tc2 = emit_ctx(*pend)
                    tf1 = self.mm(self.PS[4][0:2, 0:NT], self.E01,
                                  self.NACC[:, 0:NT], True, False,
                                  waits=[tn1, self.bank_free[4], self.t_e01] + nacc_busy,
                                  inc="tp")
                    tf2 = self.mm(self.PS[4][0:2, 0:NT], self.E10,
                                  self.NACC[:, NT:2 * NT], False, True,
                                  waits=[tn2], inc="tp")
                    qsl = slice(qs * NT, qs * NT + NT)
                    stage = self.NACC[0:2, NT:2 * NT]
                    if half == 0:
                        es = self.evac(4, lambda e, stage=stage: e.tensor_copy(
                            stage, self.PS[4][0:2, 0:NT]),
                            waits=[tf2, tn2], eng="vector")
                        tspill = self.dma(
                            "scalar",
                            self.n_spill.ap()[2 * hp: 2 * hp + 2, qsl], stage,
                            waits=[es])
                        nrow_tok[(hp, qs)] = tspill
                        nacc_busy = [es, tspill]
                        ot = self.evac(ob, lambda e, hp=hp, qs=qs, ob=ob:
                                       e.tensor_copy(
                                           self.XB[:, hp * T + qs * NT:
                                                   hp * T + qs * NT + NT],
                                           self.PS[ob][:, 0:NT]),
                                       waits=[tc1, tc2, v_ev[-1]], eng="vector")
                        xb_part_tok[(hp, qs)] = ot
                    else:
                        tload = self.dma(
                            "sync", stage,
                            self.n_spill.ap()[2 * hp: 2 * hp + 2, qsl],
                            waits=[nrow_tok[(hp, qs)], tf2] + nacc_busy)
                        rec = self.NACC[0:2, 0:NT]
                        tr_ = self.op("vector", lambda e, rec=rec, stage=stage:
                                      e.tensor_tensor(out=rec, in0=self.PS[4][0:2, 0:NT],
                                                      in1=stage, op=ALU.add),
                                      waits=[tf2, tload, tn1], inc="vp")
                        self.bank_free[4] = tr_
                        tr2 = self.op("vector", lambda e, rec=rec:
                                      e.reciprocal(out=rec, in_=rec),
                                      waits=[tr_], inc="vp")
                        rb = self.trr(PTOFF, NT)
                        tb1 = self.bcast(rec[0:1, :], rb[0:64, :], 64,
                                         waits=[tr2] + (ptbusy[0] or []) +
                                         (ptbusy[1] or []) + rb_busy)
                        tb2 = self.bcast(rec[1:2, :], rb[64:128, :], 64,
                                         waits=[tr2])
                        nacc_busy = [tr2, tb1, tb2]
                        t1 = self.op("vector", lambda e, hp=hp, qs=qs, ob=ob:
                                     e.tensor_tensor(
                                         out=self.XB[:, hp * T + qs * NT:
                                                     hp * T + qs * NT + NT],
                                         in0=self.PS[ob][:, 0:NT],
                                         in1=self.XB[:, hp * T + qs * NT:
                                                     hp * T + qs * NT + NT],
                                         op=ALU.add),
                                     waits=[tc1, tc2, xb_part_tok[(hp, qs)]],
                                     inc="vp")
                        self.bank_free[ob] = t1
                        t2 = self.op("vector", lambda e, hp=hp, qs=qs:
                                     e.tensor_tensor(
                                         out=self.XB[:, hp * T + qs * NT:
                                                     hp * T + qs * NT + NT],
                                         in0=self.XB[:, hp * T + qs * NT:
                                                     hp * T + qs * NT + NT],
                                         in1=rb, op=ALU.mult),
                                     waits=[tb1, tb2, t1], inc="vp")
                        ofinal.append(t2)
                        ofinal_map[(hp, qs)] = t2
                        ptbusy[0] = (ptbusy[0] or []) + [t2]
                        rb_busy = [t2]
                ks["busy"] = self._engine_now("tensor")
                vs["busy"] = self._engine_now("tensor")
                qbusy = self._engine_now("tensor")

        t_last = None
        for o in range(self.cfg["DT"]):
            slab = self.load_slab(self.d[f"wao{l}"][o * 128:(o + 1) * 128, :], D)
            for tc in range(TC):
                bi = tc % 2
                tokm = None
                for dc in range(DT):
                    w = ()
                    if dc == 0:
                        w = [self.tile_waitall(ofinal_map, tc, DT),
                             slab["loaded"], self.bank_free[bi]]
                    tokm = self.mm(self.PS[bi][:, 0:NT],
                                   slab["ap"][:, dc * 128:(dc + 1) * 128],
                                   self.XB[:, dc * T + tc * NT: dc * T + tc * NT + NT],
                                   dc == 0, dc == DT - 1, waits=w,
                                   inc="tp" if dc == DT - 1 else None)
                t_last = self.evac(bi, lambda e, o=o, tc=tc, bi=bi:
                                   e.scalar_tensor_tensor(
                                       out=self.XA[:, o * T + tc * NT:
                                                   o * T + tc * NT + NT],
                                       in0=self.PS[bi][:, 0:NT],
                                       scalar=self.col(f"b_ao{l}", o),
                                       in1=self.XA[:, o * T + tc * NT:
                                                   o * T + tc * NT + NT],
                                       op0=ALU.add, op1=ALU.add),
                                   waits=[tokm, self.t_cols], eng="vector")
            slab["free"].append(tokm)
        return self.emit_ln(self.XA, t_last, self.XA, f"g2{l}", f"be2{l}")

    def _max_tok(self, a, b):
        if a is None:
            return b
        if b is None:
            return a
        if a[0] == b[0]:
            return a if a[1] >= b[1] else b
        return b

    def _engine_now(self, eng):
        sem = {"tensor": "tp", "vector": "vp", "scalar": "sp", "gpsimd": "gp"}[eng]
        return (sem, self.sc[sem])

    def emit_bounds(self, l, hp, Q, q_tok, Kh, tk, PTOFF):
        c = self.cfg
        T, NT, TC, SCALE = c["T"], c["NT"], c["TC"], c["SCALE"]
        sq = self.trr(PTOFF, NT)
        toks = []
        for (src_ap, src_tok, rbase) in ((Q, q_tok, 0), (Kh, tk, 1)):
            for tc in range(TC):
                tq = self.op("scalar", lambda e, s=src_ap, tc=tc, sq=sq:
                             e.activation(out=sq, in_=s[:, tc * NT: tc * NT + NT],
                                          func=AF.Square),
                             waits=[src_tok] + toks[-1:], inc="sp")
                tm = self.mm(self.PS[1][0:2, 0:NT], self.E2, sq, True, True,
                             waits=[tq, self.bank_free[1], self.t_e2], inc="tp")
                toks.append(self.evac(1, lambda e, rbase=rbase, tc=tc: e.copy(
                    out=self.row(rbase, slice(tc * NT, tc * NT + NT), np_=2),
                    in_=self.PS[1][0:2, 0:NT]), waits=[tm]))
        qn = self.row(0, np_=2)
        kn = self.row(1, np_=2)
        qm = self.MISC[0:2, 32 + 2 * hp: 33 + 2 * hp]
        km = self.MISC[0:2, 33 + 2 * hp: 34 + 2 * hp]
        t1 = self.op("vector", lambda e: e.reduce_max(out=qm, in_=qn,
                                                      axis=mybir.AxisListType.X),
                     waits=toks, inc="vp")
        t2 = self.op("vector", lambda e: e.reduce_max(out=km, in_=kn,
                                                      axis=mybir.AxisListType.X),
                     waits=toks, inc="vp")
        t3 = self.op("vector", lambda e: e.tensor_tensor(
            out=qm, in0=qm, in1=km, op=ALU.mult), waits=[t1, t2], inc="vp")
        t4 = self.op("scalar", lambda e: e.activation(
            out=qm, in_=qm, func=AF.Sqrt, scale=(SCALE * 1.25) ** 2),
            waits=[t3], inc="sp")
        t5 = self.op("vector", lambda e: e.tensor_scalar(
            out=qm, in0=qm, scalar1=-1.0, scalar2=-2.0,
            op0=ALU.mult, op1=ALU.add), waits=[t4], inc="vp")
        tb1 = self.bcast(qm[0:1, :], self.MISC[:, 16 + 2 * hp: 17 + 2 * hp], 128,
                         waits=[t5])
        tb2 = self.bcast(qm[1:2, :], self.MISC[:, 17 + 2 * hp: 18 + 2 * hp], 128,
                         waits=[t5])
        return self._max_tok(tb1, tb2)

    def emit_ffn(self, l, x2_tok):
        c = self.cfg
        T, D, NT, TC, DT, DFT = c["T"], c["D"], c["NT"], c["TC"], c["DT"], c["DFT"]
        t_last = None
        x3map = {}
        h_tok = {}
        for tc in range(TC):
            for os_ in range(DFT):
                slab = self.load_slab(self.d[f"wf1{l}"][os_ * 128:(os_ + 1) * 128, :], D)
                bi = os_ % 2
                tokm = None
                for dc in range(DT):
                    w = ()
                    if dc == 0:
                        w = [self.tile_waitall(x2_tok, tc, DT),
                             slab["loaded"], self.bank_free[bi]]
                    tokm = self.mm(self.PS[bi][:, 0:NT],
                                   slab["ap"][:, dc * 128:(dc + 1) * 128],
                                   self.XA[:, dc * T + tc * NT: dc * T + tc * NT + NT],
                                   dc == 0, dc == DT - 1, waits=w,
                                   inc="tp" if dc == DT - 1 else None)
                slab["free"].append(tokm)
                h_tok[os_] = self.evac(bi, lambda e, os_=os_, bi=bi: e.activation(
                    out=self.XB[:, os_ * NT: os_ * NT + NT], in_=self.PS[bi][:, 0:NT],
                    func=AF.Gelu, bias=self.col(f"b_f1{l}", os_)),
                    waits=[tokm, self.t_cols])
            for o in range(DT):
                bi = 2 + o % 2
                tokm = None
                nq4 = (DFT * 128) // 1024
                for q4 in range(nq4):
                    slab = self.load_slab(
                        self.d[f"wf2{l}"][o * 128:(o + 1) * 128,
                                          q4 * 1024:(q4 + 1) * 1024], 1024)
                    for j in range(8):
                        dc = q4 * 8 + j
                        w = [slab["loaded"], h_tok[dc]]
                        if q4 == 0 and j == 0:
                            w.append(self.bank_free[bi])
                        tokm = self.mm(self.PS[bi][:, 0:NT],
                                       slab["ap"][:, j * 128:(j + 1) * 128],
                                       self.XB[:, dc * NT: dc * NT + NT],
                                       dc == 0, dc == DFT - 1, waits=w,
                                       inc="tp" if (j == 7 or dc == DFT - 1)
                                       else None)
                    slab["free"].append(tokm)
                t_last = self.evac(bi, lambda e, o=o, tc=tc, bi=bi:
                                   e.scalar_tensor_tensor(
                                       out=self.XA[:, o * T + tc * NT:
                                                   o * T + tc * NT + NT],
                                       in0=self.PS[bi][:, 0:NT],
                                       scalar=self.col(f"b_f2{l}", o),
                                       in1=self.XA[:, o * T + tc * NT:
                                                   o * T + tc * NT + NT],
                                       op0=ALU.add, op1=ALU.add),
                                   waits=[tokm, self.t_cols], eng="vector")
                x3map[(o, tc)] = t_last
        x3map["last"] = t_last
        return x3map

    def emit_tail(self, x16_tok):
        c = self.cfg
        T, D, NT, TC, DT = c["T"], c["D"], c["NT"], c["TC"], c["DT"]
        if c["with_tail"]:
            assert D == 1024, "TT tail assumes D=1024"
            WrO, WiO, WnO = 0, 2048, 2 * 2048
            scr = 3 * 2048
            tw = self.emit_ttw(scr, WrO, WiO, WnO)
        ring = [{"off": 7680, "busy": None}, {"off": 7680 + T, "busy": None}] \
            if c["with_tail"] else \
            [{"off": 0, "busy": None}, {"off": T, "busy": None}]
        t_mid = None
        midmap = {}
        for o in range(DT):
            r = ring[o % 2]
            xt = self.trr(r["off"], T)
            tl = self.dma("sync", xt, self.d["x_in"][o * 128:(o + 1) * 128, :],
                          waits=[r["busy"]])
            t_mid = self.op("vector", lambda e, o=o, xt=xt: e.tensor_tensor(
                out=self.XA[:, o * T:(o + 1) * T],
                in0=self.XA[:, o * T:(o + 1) * T], in1=xt, op=ALU.add),
                waits=[tl] + [self.tmap_get(x16_tok, o, tc)
                              for tc in range(TC)], inc="vp")
            for tc in range(TC):
                midmap[(o, tc)] = t_mid
            r["busy"] = t_mid
        if not c["with_tail"]:
            midmap["last"] = t_mid
            return midmap
        xh_tok = self.emit_ln(self.XA, midmap, self.XB, None, None,
                              bc_waits=[tw] if c["with_tail"] else ())
        t_last = None
        for ri in range(2):
            for ot in range(4):
                for tc in range(TC):
                    bi = tc % 2
                    mms = []
                    if ri == 0:
                        mats = [(WrO, 0), (WnO, 4)]
                    else:
                        mats = [(WiO, 0), (WrO, 4)]
                    tokm = None
                    first = True
                    for mo, dbase in mats:
                        for dc4 in range(4):
                            lhsT = self.TR[:, mo + dc4 * 512 + ot * 128:
                                           mo + dc4 * 512 + ot * 128 + 128]
                            rhs = self.XB[:, (dbase + dc4) * T + tc * NT:
                                          (dbase + dc4) * T + tc * NT + NT]
                            w = ()
                            if first:
                                w = [self.tile_waitall(xh_tok, tc, DT),
                                     tw, self.bank_free[bi]]
                            tokm = self.mm(self.PS[bi][:, 0:NT], lhsT, rhs,
                                           first, (mo, dc4) == (mats[1][0], 3),
                                           waits=w,
                                           inc="tp" if (mo, dc4) == (mats[1][0], 3)
                                           else None)
                            first = False
                    o_out = ri * 4 + ot
                    fb = "fb_r" if ri == 0 else "fb_i"
                    t_last = self.evac(bi, lambda e, o_out=o_out, tc=tc, bi=bi,
                                       fb=fb, ot=ot: e.scalar_tensor_tensor(
                                           out=self.XA[:, o_out * T + tc * NT:
                                                       o_out * T + tc * NT + NT],
                                           in0=self.PS[bi][:, 0:NT],
                                           scalar=self.col(fb, ot),
                                           in1=self.XA[:, o_out * T + tc * NT:
                                                       o_out * T + tc * NT + NT],
                                           op0=ALU.add, op1=ALU.add),
                                       waits=[tokm, self.t_cols], eng="vector")
        return t_last

    def emit_ttw(self, scr, WrO, WiO, WnO):
        R8T = self.trr(scr, 128)
        R16T = self.trr(scr + 128, 128)
        ACT = self.trr(scr + 256, 32)
        ZC = self.trr(scr + 288, 16)
        TMP = self.trr(scr + 320, 512)
        G1 = self.trr(scr + 832, 512)
        G2 = self.trr(scr + 1344, 64)
        t_r8 = self.dma("sync", R8T[0:8, :], self.d["r8r16"][0:8, :])
        t_r16 = self.dma("sync", R16T[0:16, :], self.d["r8r16"][8:24, :])
        t_con = [t_r8, t_r16]
        tw_done = []
        g_busy = []
        for mi, (g1n, g2n, WO) in enumerate((("g1r", "g2r", WrO),
                                             ("g1i", "g2i", WiO))):
            tg1 = self.dma("sync", G1[0:8, :], self.d[g1n], waits=g_busy)
            tg2 = self.dma("sync", G2[0:16, :], self.d[g2n], waits=g_busy)
            for t in range(4):
                wt = self.trr(WO + t * 512, 512)
                for cc in range(4):
                    ta = self.mm(self.PS[2][:, 0:32], R8T[0:8, :],
                                 G1[0:8, (t * 4 + cc) * 32:(t * 4 + cc) * 32 + 32],
                                 True, True,
                                 waits=[tg1, t_con, self.bank_free[2]], inc="tp")
                    ea = self.evac(2, lambda e, ta=ta: e.copy(out=ACT,
                                   in_=self.PS[2][:, 0:32]), waits=[ta])
                    tzc = self.mm(self.PS[3][:, 0:16], R16T[0:16, :],
                                  G2[0:16, cc * 16: cc * 16 + 16],
                                  True, True,
                                  waits=[tg2, t_con, self.bank_free[3]], inc="tp")
                    ez = self.evac(3, lambda e, tzc=tzc: e.copy(out=ZC,
                                   in_=self.PS[3][:, 0:16]), waits=[tzc])
                    xap = ACT.unsqueeze(2).broadcast_to([128, 32, 16])
                    yap = ZC.unsqueeze(1).broadcast_to([128, 32, 16])
                    dst = wt if cc == 0 else TMP
                    tm = self.op("vector", lambda e, d=dst, xap=xap, yap=yap:
                                 e.tensor_tensor(
                                     out=d.rearrange("p (b e) -> p b e", b=32),
                                     in0=xap, in1=yap, op=ALU.mult),
                                 waits=[ea, ez], inc="vp")
                    if cc > 0:
                        tm = self.op("vector", lambda e, wt=wt: e.tensor_tensor(
                            out=wt, in0=wt, in1=TMP, op=ALU.add),
                            waits=[tm], inc="vp")
                    self.bank_free[2] = tm
                    self.bank_free[3] = tm
                    g_busy += [ta, tzc]
                tw_done.append(tm)
        for t in range(4):
            tw_done.append(self.op(
                "scalar", lambda e, t=t: e.mul(
                    out=self.trr(WnO + t * 512, 512),
                    in_=self.trr(WiO + t * 512, 512), mul=-1.0),
                waits=[tw_done[-1]], inc="sp"))
        return tw_done[-1]


def _ref_weight_dict(cfg, inputs):
    L = cfg["L"]
    wd = {}
    for l in range(L):
        for src, dst in (("W_in", "W_in"), ("W_dt", "W_dt"), ("W_xp", "W_xp"),
                         ("W_out", "W_out"), ("W_qkv", "W_qkv"), ("W_ao", "W_ao"),
                         ("W_f1", "W_f1"), ("W_f2", "W_f2"),
                         ("b_in", "b_in"), ("b_dt", "b_dt"), ("b_xp", "b_xp"),
                         ("b_out", "b_out"), ("b_qkv", "b_qkv"), ("b_ao", "b_ao"),
                         ("b_f1", "b_f1"), ("b_f2", "b_f2"),
                         ("g1", "g1"), ("be1", "be1"), ("g2", "g2"), ("be2", "be2"),
                         ("Dp", "Dp"), ("A_log", "Alog")):
            wd[f"{dst}{l}"] = np.asarray(inputs[f"L0_attn_{src}"])[l]
    wd["G1r"] = np.asarray(inputs["L0_ffn_G1_R"])
    wd["G2r"] = np.asarray(inputs["L0_ffn_G2_R"])
    wd["G1i"] = np.asarray(inputs["L0_ffn_G1_I"])
    wd["G2i"] = np.asarray(inputs["L0_ffn_G2_I"])
    wd["fbias_r"] = np.asarray(inputs["L0_ffn_bias_real"])
    wd["fbias_i"] = np.asarray(inputs["L0_ffn_bias_imag"])
    return wd


def run_cores(cfg, wd, x_full, stage="full"):
    B, S, D = x_full.shape
    T = cfg["T"]
    assert S == 2 * T and B * 2 == 8
    wm, colidx = host_prepare_weights(cfg, wd)
    cfg = dict(cfg)
    cfg["ncol"] = wm["colsimg"].shape[1]
    cfg["colidx"] = colidx
    bld = Builder(cfg, colidx, stage=stage)
    nc = bld.build()
    in_maps = []
    for core in range(8):
        b_, hh = core // 2, core % 2
        m = dict(wm)
        xslice = np.asarray(x_full[b_, hh * T:(hh + 1) * T, :], np.float32)
        m["x_in"] = np.ascontiguousarray(xslice.T)
        m["maskcol"] = np.full((cfg["NST"], 1), float(hh), np.float32)
        m["pinfo"] = np.full((1, 1), 1 - hh, np.int32)
        in_maps.append(m)
    res = run_bass_kernel_spmd(nc, in_maps, core_ids=list(range(8)))
    out = np.empty((B, S, D), np.float32)
    for core in range(8):
        b_, hh = core // 2, core % 2
        out[b_, hh * T:(hh + 1) * T, :] = res.results[core]["y"].T
    return out


def kernel(**inputs):
    cfg = make_cfg(D=1024, T=2048, H=16, L=4)
    wd = _ref_weight_dict(cfg, inputs)
    x_full = np.asarray(inputs["L0_attn_x"], np.float32)
    return run_cores(cfg, wd, x_full, stage="full")
